# version 15 (final)
# baseline (speedup 1.0000x reference)
import numpy as np
from contextlib import ExitStack

import concourse.bass as bass
import concourse.mybir as mybir
from concourse.bass_utils import run_bass_kernel_spmd
from concourse.alu_op_type import AluOpType as ALU

F32 = mybir.dt.float32
BF16 = mybir.dt.bfloat16
AF = mybir.ActivationFunctionType
AX = mybir.AxisListType

S, E, FQKV = 2048, 4096, 6144
HQ, HKV, D, GRP = 32, 8, 128, 4
NEC = 32
NFC_QKV, NFC_OUT = 48, 32
NEXP, TOPK, RANK, BOT = 64, 16, 8, 512
NNC = 4
SCALE_QK = 1.0 / np.sqrt(128.0)
EPS = 1e-5
CHA = 512
NCHA = S // CHA
QT = 512
NQT = S // QT
CHC = 512
NCHC = S // CHC
CHD = 512
NCHD = S // CHD

ENGS = ("sync", "tensor", "vector", "scalar", "gpsimd")


class Sch:

    def __init__(self):
        self.ops = {e: [] for e in ENGS}
        self.cnt = {}
        self.res = {}
        for e in ENGS:
            self.cnt[e] = 0

    def emit(self, eng, fn):
        self.ops[eng].append(("op", fn))

    def sig(self, eng, fn, sem=None, n=1):
        sem = sem or eng
        self.cnt.setdefault(sem, 0)
        self.ops[eng].append(("sig", fn, sem, n))
        self.cnt[sem] += n
        return self.cnt[sem]

    def dma(self, eng, fn, sem):
        return self.sig(eng, fn, sem, 16)

    def wait(self, eng, sem, val):
        if val > 0 and sem != eng:
            self.ops[eng].append(("wait", sem, val))

    def use(self, eng, resname):
        ev = self.res.get(resname)
        if ev is not None:
            self.wait(eng, ev[0], ev[1])

    def setr(self, resname, sem):
        self.res[resname] = (sem, self.cnt[sem])

    def barrier(self):
        snap = dict(self.cnt)
        for e in ENGS:
            for o, v in snap.items():
                if o != e and v > 0:
                    self.wait(e, o, v)

    def playback(self, block, sems):
        for engname in ENGS:
            ops = self.ops[engname]

            def body(eng, _ops=ops):
                for rec in _ops:
                    if rec[0] == "op":
                        rec[1](eng)
                    elif rec[0] == "sig":
                        rec[1](eng).then_inc(sems[rec[2]], rec[3])
                    else:
                        eng.wait_ge(sems[rec[1]], rec[2])

            getattr(block, engname)(body)


def build(dbg=False):
    nc = bass.Bass("TRN2", target_bir_lowering=False)
    xT = nc.dram_tensor("xT", [E, S], F32, kind="ExternalInput").ap()
    wqkv = nc.dram_tensor("wqkv", [NFC_QKV, 128, NEC, 128], F32, kind="ExternalInput").ap()
    wout = nc.dram_tensor("wout", [NFC_OUT, 128, NEC, 128], F32, kind="ExternalInput").ap()
    wa = nc.dram_tensor("wa", [NNC, 128, NEC, 128], F32, kind="ExternalInput").ap()
    wb = nc.dram_tensor("wb", [NFC_OUT, 128, NNC, 128], F32, kind="ExternalInput").ap()
    wqT = nc.dram_tensor("wqT", [E, 16], F32, kind="ExternalInput").ap()
    keysT = nc.dram_tensor("keysT", [16, NEXP], F32, kind="ExternalInput").ap()
    ropeQ = nc.dram_tensor("ropeQ", [2, 2, 128, S], F32, kind="ExternalInput").ap()
    ropeK = nc.dram_tensor("ropeK", [2, 2, 128, S], F32, kind="ExternalInput").ap()
    stair = nc.dram_tensor("stair", [128, 896], F32, kind="ExternalInput").ap()
    emat = nc.dram_tensor("emat", [64, BOT], F32, kind="ExternalInput").ap()
    ident = nc.dram_tensor("ident", [128, 128], F32, kind="ExternalInput").ap()
    outT = nc.dram_tensor("outT", [E, S], F32, kind="ExternalOutput").ap()
    kscr = "ExternalOutput" if dbg else "Internal"
    qT_hi = nc.dram_tensor("qT_hi", [E, S], BF16, kind=kscr).ap()
    qT_lo = nc.dram_tensor("qT_lo", [E, S], BF16, kind=kscr).ap()
    kT_hi = nc.dram_tensor("kT_hi", [HKV * D, S], BF16, kind=kscr).ap()
    kT_lo = nc.dram_tensor("kT_lo", [HKV * D, S], BF16, kind=kscr).ap()
    v_hi = nc.dram_tensor("v_hi", [S, HKV * D], BF16, kind=kscr).ap()
    v_lo = nc.dram_tensor("v_lo", [S, HKV * D], BF16, kind=kscr).ap()
    attnT_scr = nc.dram_tensor("attnT_scr", [E, S], F32, kind=kscr).ap()
    wqkv_hi = nc.dram_tensor("wqkv_hi", [NFC_QKV, 128, NEC * 128], BF16, kind="Internal").ap()
    wqkv_lo = nc.dram_tensor("wqkv_lo", [NFC_QKV, 128, NEC * 128], BF16, kind="Internal").ap()
    wout_hi = nc.dram_tensor("wout_hi", [NFC_OUT, 128, NEC * 128], BF16, kind="Internal").ap()
    wout_lo = nc.dram_tensor("wout_lo", [NFC_OUT, 128, NEC * 128], BF16, kind="Internal").ap()
    x1T_scr = nc.dram_tensor("x1T_scr", [E, S], F32, kind=kscr).ap()

    sch = Sch()
    with ExitStack() as ctx:
        sid = ctx.enter_context(nc.sbuf_tensor("sid", [128, 128], F32))
        sstair = ctx.enter_context(nc.sbuf_tensor("sstair", [128, 896], F32))
        semat = ctx.enter_context(nc.sbuf_tensor("semat", [64, BOT], F32))
        skeys = ctx.enter_context(nc.sbuf_tensor("skeys", [16, NEXP], F32))
        swq = ctx.enter_context(nc.sbuf_tensor("swq", [128, NEC * 16], F32))
        sones = ctx.enter_context(nc.sbuf_tensor("sones", [128, 128], F32))
        seps = ctx.enter_context(nc.sbuf_tensor("seps", [128, 1], F32))
        sonesb = ctx.enter_context(nc.sbuf_tensor("sonesb", [128, 2], BF16))
        PS = [ctx.enter_context(nc.psum_tensor("ps%d" % i, [128, 512], F32)) for i in range(8)]

        sch.dma("sync", lambda e: e.dma_start(out=sid[:], in_=ident[:]), "cN")
        sch.dma("sync", lambda e: e.dma_start(out=sstair[:], in_=stair[:]), "cN")
        sch.dma("sync", lambda e: e.dma_start(out=semat[:], in_=emat[:]), "cN")
        sch.dma("sync", lambda e: e.dma_start(out=skeys[:], in_=keysT[:]), "cN")
        sch.dma("sync", lambda e: e.dma_start(
            out=swq[:].rearrange("p (g r) -> p g r", r=16),
            in_=wqT[:, :].rearrange("(g p) r -> p g r", p=128)), "cN")
        ld_consts = sch.cnt["cN"]
        sch.emit("vector", lambda e: e.memset(sones[:], 1.0))
        sch.emit("vector", lambda e: e.memset(sonesb[:], 1.0))
        sch.sig("vector", lambda e: e.memset(seps[:], EPS))
        ve_ones = sch.cnt["vector"]

        with ExitStack() as actx:
            xstg = [actx.enter_context(nc.sbuf_tensor("xstgA%d" % i, [128, 4 * CHA], F32)) for i in range(2)]
            xhi = actx.enter_context(nc.sbuf_tensor("xhiA", [128, NEC * CHA], BF16))
            xlo = actx.enter_context(nc.sbuf_tensor("xloA", [128, NEC * CHA], BF16))
            wstg = [actx.enter_context(nc.sbuf_tensor("wstgA%d" % i, [128, NEC * 128], F32)) for i in range(2)]
            whi = [actx.enter_context(nc.sbuf_tensor("whiA%d" % i, [128, NEC * 128], BF16)) for i in range(2)]
            wlo = [actx.enter_context(nc.sbuf_tensor("wloA%d" % i, [128, NEC * 128], BF16)) for i in range(2)]
            rQ = [[actx.enter_context(nc.sbuf_tensor("rQ%d%d" % (a, b), [128, CHA], F32)) for b in range(2)] for a in range(2)]
            rK = [[actx.enter_context(nc.sbuf_tensor("rK%d%d" % (a, b), [128, CHA], F32)) for b in range(2)] for a in range(2)]
            qkst = [actx.enter_context(nc.sbuf_tensor("qkstA%d" % i, [128, CHA], F32)) for i in range(2)]
            vst = [actx.enter_context(nc.sbuf_tensor("vstA%d" % i, [128, CHA], F32)) for i in range(2)]
            vtk = [actx.enter_context(nc.sbuf_tensor("vtkA%d" % i, [128, CHA], F32)) for i in range(2)]
            tmpA = [actx.enter_context(nc.sbuf_tensor("tmpA%d" % i, [128, CHA], F32)) for i in range(8)]
            qkh = [actx.enter_context(nc.sbuf_tensor("qkhA%d" % i, [128, CHA], BF16)) for i in range(2)]
            qkl = [actx.enter_context(nc.sbuf_tensor("qklA%d" % i, [128, CHA], BF16)) for i in range(2)]
            vhs = [actx.enter_context(nc.sbuf_tensor("vhsA%d" % i, [128, CHA], BF16)) for i in range(2)]
            vls = [actx.enter_context(nc.sbuf_tensor("vlsA%d" % i, [128, CHA], BF16)) for i in range(2)]

            def emit_rope(pend):
                Lb, Hb, rt, j, scr_hi, scr_lo, ev_mm, ev_r, t0p = pend
                sch.wait("vector", "tensor", ev_mm)
                sch.wait("vector", "rA", ev_r)
                sch.use("vector", "qkstA0")
                sch.use("vector", "qkstA1")
                def rope(e, _L=Lb, _H=Hb, _rt=rt):
                    L = _L[:, 0:CHA]; H = _H[:, 0:CHA]
                    dlo = qkst[0][:]; dhi = qkst[1][:]
                    t = [x[:] for x in tmpA]
                    e.tensor_tensor(out=t[0], in0=L, in1=_rt[0][0][:], op=ALU.mult)
                    e.tensor_tensor(out=t[4], in0=H, in1=_rt[0][0][:], op=ALU.mult)
                    e.tensor_tensor(out=t[1], in0=H, in1=_rt[0][1][:], op=ALU.mult)
                    e.tensor_tensor(out=t[5], in0=L, in1=_rt[0][1][:], op=ALU.mult)
                    e.tensor_tensor(out=t[2], in0=L, in1=_rt[1][0][:], op=ALU.mult)
                    e.tensor_tensor(out=t[6], in0=H, in1=_rt[1][0][:], op=ALU.mult)
                    e.tensor_tensor(out=t[3], in0=H, in1=_rt[1][1][:], op=ALU.mult)
                    e.tensor_tensor(out=t[7], in0=L, in1=_rt[1][1][:], op=ALU.mult)
                    e.drain()
                    e.tensor_tensor(out=dlo, in0=t[0], in1=t[1], op=ALU.subtract)
                    e.tensor_tensor(out=dhi, in0=t[4], in1=t[5], op=ALU.add)
                    e.drain()
                    e.tensor_tensor(out=t[1], in0=t[2], in1=t[3], op=ALU.subtract)
                    e.tensor_tensor(out=t[5], in0=t[6], in1=t[7], op=ALU.add)
                    e.drain()
                    e.tensor_tensor(out=dlo, in0=dlo, in1=t[1], op=ALU.add)
                    return e.tensor_tensor(out=dhi, in0=dhi, in1=t[5], op=ALU.add)
                sch.sig("vector", rope)
                sch.setr("psA%d" % PS.index(Lb), "vector")
                sch.setr("psA%d" % PS.index(Hb), "vector")
                ev_rope = sch.cnt["vector"]
                for half in range(2):
                    sch.wait("scalar", "vector", ev_rope)
                    sch.use("scalar", "qkhA%d" % half)
                    def qcast(e, _h=half):
                        return e.copy(out=qkh[_h][:], in_=qkst[_h][:])
                    sch.sig("scalar", qcast)
                    ev_qc = sch.cnt["scalar"]
                    sch.wait("vector", "scalar", ev_qc)
                    def qsub(e, _h=half):
                        e.drain()
                        return e.tensor_tensor(out=qkl[_h][:], in0=qkst[_h][:], in1=qkh[_h][:], op=ALU.subtract)
                    sch.sig("vector", qsub)
                    ev_qs = sch.cnt["vector"]
                    sch.setr("qkstA%d" % half, "vector")
                    sch.wait("gpsimd", "vector", ev_qs)
                    for (buf, scrp) in ((qkh, scr_hi), (qkl, scr_lo)):
                        for hh in range(2):
                            def stq(e, _j=j, _half=half, _hh=hh, _b=buf, _s=scrp, _t0=t0p):
                                r0 = (_j * 2 + _hh) * 128 + _half * 64
                                return e.dma_start(out=_s[r0:r0 + 64, _t0:_t0 + CHA],
                                                   in_=_b[_half][_hh * 64:(_hh + 1) * 64, :])
                            sch.dma("gpsimd", stq, "sQK%d" % half)
                    sch.setr("qkhA%d" % half, "sQK%d" % half)

            for c in range(NCHA):
                t0 = c * CHA
                pending = None
                ev_xsplit_g = {}
                for grp_ in range(8):
                    xs = xstg[grp_ % 2]
                    sch.use("sync", "xstgA%d" % (grp_ % 2))
                    if grp_ < 2:
                        sch.use("sync", "xpairA")
                    def ldx(e, _g=grp_, _t0=t0, _xs=xs):
                        src = xT[_g * 512:(_g + 1) * 512, _t0:_t0 + CHA].rearrange("(g p) t -> p g t", p=128)
                        return e.dma_start(out=_xs[:].rearrange("p (g t) -> p g t", t=CHA), in_=src)
                    sch.dma("sync", ldx, "xA%d" % (grp_ % 2))
                    ev_ldx = sch.cnt["xA%d" % (grp_ % 2)]
                    sch.wait("scalar", "xA%d" % (grp_ % 2), ev_ldx)
                    def xcast(e, _g=grp_, _xs=xs):
                        return e.copy(out=xhi[:, _g * 4 * CHA:(_g + 1) * 4 * CHA], in_=_xs[:])
                    sch.sig("scalar", xcast)
                    ev_xc = sch.cnt["scalar"]
                    sch.wait("vector", "scalar", ev_xc)
                    def xsub(e, _g=grp_, _xs=xs):
                        return e.tensor_tensor(out=xlo[:, _g * 4 * CHA:(_g + 1) * 4 * CHA], in0=_xs[:],
                                               in1=xhi[:, _g * 4 * CHA:(_g + 1) * 4 * CHA], op=ALU.subtract)
                    sch.sig("vector", xsub)
                    sch.setr("xstgA%d" % (grp_ % 2), "vector")
                    ev_xsplit_g[grp_] = sch.cnt["vector"]
                ev_xsplit = sch.cnt["vector"]

                for hl in range(2):
                    for cs in range(2):
                        def ldr(e, _h=hl, _c=cs, _t0=t0):
                            return e.dma_start(out=rQ[_h][_c][:], in_=ropeQ[_h, _c][:, _t0:_t0 + CHA])
                        sch.dma("sync", ldr, "rA")
                        def ldr2(e, _h=hl, _c=cs, _t0=t0):
                            return e.dma_start(out=rK[_h][_c][:], in_=ropeK[_h, _c][:, _t0:_t0 + CHA])
                        sch.dma("sync", ldr2, "rA")
                ev_r = sch.cnt["rA"]

                for f in range(NFC_QKV):
                    g = c * NFC_QKV + f
                    wslot = g % 2
                    bank = g % 4
                    if c == 0:
                        sch.use("sync", "wstgA%d" % wslot)
                        def ldw(e, _f=f, _w=wstg[wslot]):
                            return e.dma_start(out=_w[:], in_=wqkv[_f].rearrange("p a b -> p (a b)"))
                        sch.dma("sync", ldw, "wA%d" % wslot)
                        ev_w = sch.cnt["wA%d" % wslot]
                        sch.wait("scalar", "wA%d" % wslot, ev_w)
                        sch.use("scalar", "wpairA%d" % wslot)
                        sch.use("scalar", "wpairAs%d" % wslot)
                        def wcast(e, _s=wslot):
                            return e.copy(out=whi[_s][:], in_=wstg[_s][:])
                        sch.sig("scalar", wcast)
                        ev_whi = sch.cnt["scalar"]
                        sch.wait("vector", "scalar", ev_whi)
                        def wsub(e, _s=wslot):
                            return e.tensor_tensor(out=wlo[_s][:], in0=wstg[_s][:], in1=whi[_s][:], op=ALU.subtract)
                        sch.sig("vector", wsub)
                        ev_wlo = sch.cnt["vector"]
                        sch.setr("wstgA%d" % wslot, "vector")
                        sch.wait("gpsimd", "vector", ev_wlo)
                        def stwh(e, _f=f, _s=wslot):
                            return e.dma_start(out=wqkv_hi[_f], in_=whi[_s][:])
                        sch.dma("gpsimd", stwh, "sWA%d" % wslot)
                        def stwl(e, _f=f, _s=wslot):
                            return e.dma_start(out=wqkv_lo[_f], in_=wlo[_s][:])
                        sch.dma("gpsimd", stwl, "sWA%d" % wslot)
                        sch.setr("wpairAs%d" % wslot, "sWA%d" % wslot)
                        wdep = ("vector", max(ev_xsplit, ev_wlo))
                    else:
                        sch.use("sync", "wpairA%d" % wslot)
                        sch.use("sync", "wpairAs%d" % wslot)
                        def ldwh(e, _f=f, _s=wslot):
                            return e.dma_start(out=whi[_s][:], in_=wqkv_hi[_f])
                        sch.dma("sync", ldwh, "wA%d" % wslot)
                        def ldwl(e, _f=f, _s=wslot):
                            return e.dma_start(out=wlo[_s][:], in_=wqkv_lo[_f])
                        sch.dma("sync", ldwl, "wA%d" % wslot)
                        ev_w = sch.cnt["wA%d" % wslot]
                        sch.setr("wpairAs%d" % wslot, "wA%d" % wslot)
                        wdep = ("wA%d" % wslot, ev_w)

                    if pending is not None and (f % 2 == 0):
                        emit_rope(pending)
                        pending = None

                    sch.wait("tensor", wdep[0], wdep[1])
                    sch.use("tensor", "psA%d" % bank)
                    if c == 0 and f == 0:
                        sch.wait("tensor", "vector", ve_ones)
                    pbank = PS[bank]
                    nmm = NEC * 3
                    k = 0
                    for ec in range(NEC):
                        if ec % 4 == 0 and f == 0:
                            sch.wait("tensor", "vector", ev_xsplit_g[ec // 4])
                        for (wo, xo) in ((whi[wslot], xhi), (whi[wslot], xlo), (wlo[wslot], xhi)):
                            k += 1
                            def mm(e, _p=pbank, _w=wo, _x=xo, _ec=ec, _s=(k == 1), _e=(k == nmm)):
                                return e.matmul(_p[:, 0:CHA], _w[:, _ec * 128:(_ec + 1) * 128],
                                                _x[:, _ec * CHA:(_ec + 1) * CHA], start=_s, stop=_e)
                            if k == nmm:
                                sch.sig("tensor", mm)
                            else:
                                sch.emit("tensor", mm)
                    ev_mm = sch.cnt["tensor"]
                    sch.setr("wpairA%d" % wslot, "tensor")
                    if f == NFC_QKV - 1:
                        sch.setr("xpairA", "tensor")

                    if f < HQ + HKV:
                        if f % 2 == 1:
                            isq = f < HQ
                            rt = rQ if isq else rK
                            j = (f // 2) if isq else ((f - HQ) // 2)
                            scrh = qT_hi if isq else kT_hi
                            scrl = qT_lo if isq else kT_lo
                            pending = (PS[(bank + 3) % 4], PS[bank], rt, j, scrh, scrl, ev_mm, ev_r, t0)
                    else:
                        if pending is not None:
                            emit_rope(pending)
                            pending = None
                        sch.wait("scalar", "tensor", ev_mm)
                        sch.use("scalar", "vstA%d" % wslot)
                        def vcopy(e, _p=pbank, _v=vst[wslot]):
                            return e.copy(out=_v[:], in_=_p[:, 0:CHA])
                        sch.sig("scalar", vcopy)
                        sch.setr("psA%d" % bank, "scalar")
                        ev_vc = sch.cnt["scalar"]
                        sch.wait("tensor", "scalar", ev_vc)
                        sch.use("tensor", "psVT%d" % wslot)
                        if c == 0 and f == HQ + HKV:
                            sch.wait("tensor", "cN", ld_consts)
                        ptb = PS[4 + wslot]
                        for tb in range(CHA // 128):
                            def vtr(e, _p=ptb, _v=vst[wslot], _tb=tb):
                                return e.transpose(_p[:, _tb * 128:(_tb + 1) * 128],
                                                   _v[:, _tb * 128:(_tb + 1) * 128], sid[:])
                            if tb == CHA // 128 - 1:
                                sch.sig("tensor", vtr)
                            else:
                                sch.emit("tensor", vtr)
                        sch.setr("vstA%d" % wslot, "tensor")
                        ev_vt = sch.cnt["tensor"]
                        sch.wait("vector", "tensor", ev_vt)
                        sch.use("vector", "vtkA%d" % wslot)
                        def vevac(e, _p=ptb, _v=vtk[wslot]):
                            return e.tensor_copy(_v[:], _p[:, 0:CHA])
                        sch.sig("vector", vevac)
                        sch.setr("psVT%d" % wslot, "vector")
                        ev_ve = sch.cnt["vector"]
                        sch.wait("scalar", "vector", ev_ve)
                        sch.use("scalar", "vpairA%d" % wslot)
                        def vcast2(e, _s=wslot):
                            return e.copy(out=vhs[_s][:], in_=vtk[_s][:])
                        sch.sig("scalar", vcast2)
                        ev_vc2 = sch.cnt["scalar"]
                        sch.wait("vector", "scalar", ev_vc2)
                        def vsub2(e, _s=wslot):
                            e.drain()
                            return e.tensor_tensor(out=vls[_s][:], in0=vtk[_s][:], in1=vhs[_s][:], op=ALU.subtract)
                        sch.sig("vector", vsub2)
                        ev_vs2 = sch.cnt["vector"]
                        sch.setr("vtkA%d" % wslot, "vector")
                        sch.wait("gpsimd", "vector", ev_vs2)
                        for (vb, vscr) in ((vhs, v_hi), (vls, v_lo)):
                            def stv(e, _v=vb[wslot], _f=f - HQ - HKV, _t0=t0, _s=vscr):
                                dst = _s[_t0:_t0 + CHA, _f * 128:(_f + 1) * 128].rearrange("(g p) d -> p g d", p=128)
                                return e.dma_start(out=dst, in_=_v[:].rearrange("p (g d) -> p g d", d=128))
                            sch.dma("gpsimd", stv, "sV%d" % wslot)
                        sch.setr("vpairA%d" % wslot, "sV%d" % wslot)
                if pending is not None:
                    emit_rope(pending)
                    pending = None

        sch.barrier()
        sch.res.clear()

        with ExitStack() as bctx:
            KTh = bctx.enter_context(nc.sbuf_tensor("KTh", [128, HKV * S], BF16))
            KTl = bctx.enter_context(nc.sbuf_tensor("KTl", [128, HKV * S], BF16))
            Vh = bctx.enter_context(nc.sbuf_tensor("Vh", [128, (S // 128) * HKV * D], BF16))
            Vl = bctx.enter_context(nc.sbuf_tensor("Vl", [128, (S // 128) * HKV * D], BF16))
            pF = [bctx.enter_context(nc.sbuf_tensor("pF%d" % i, [128, QT], F32)) for i in range(2)]
            ph = [bctx.enter_context(nc.sbuf_tensor("phB%d" % i, [128, 8 * QT], BF16)) for i in range(2)]
            pl = [bctx.enter_context(nc.sbuf_tensor("plB%d" % i, [128, 8 * QT], BF16)) for i in range(2)]
            fin_heads = []
            fin_q0 = []
            qbh = [bctx.enter_context(nc.sbuf_tensor("qbhB%d" % i, [128, QT], BF16)) for i in range(2)]
            qbl = [bctx.enter_context(nc.sbuf_tensor("qblB%d" % i, [128, QT], BF16)) for i in range(2)]
            srec = bctx.enter_context(nc.sbuf_tensor("srecB", [1, QT], F32))
            sbc = bctx.enter_context(nc.sbuf_tensor("sbcB", [128, QT], F32))
            atst = [bctx.enter_context(nc.sbuf_tensor("atstB%d" % i, [128, QT], F32)) for i in range(2)]

            def finalize(pend):
                qh, pslot, ev_rec, ev_pv, q0 = pend
                sch.wait("tensor", "vector", ev_rec)
                sch.use("tensor", "psSC3")
                sch.sig("tensor", lambda e: e.matmul(PS[3][:, 0:QT], sones[0:1, 0:128], srec[0:1, :],
                                                     start=True, stop=True))
                ev_bc = sch.cnt["tensor"]
                sch.setr("srecB", "tensor")
                sch.wait("scalar", "tensor", ev_bc)
                sch.use("scalar", "sbcB")
                sch.sig("scalar", lambda e: e.copy(out=sbc[:], in_=PS[3][:, 0:QT]))
                ev_bce = sch.cnt["scalar"]
                sch.setr("psSC3", "scalar")
                sch.wait("vector", "scalar", ev_bce)
                sch.wait("vector", "tensor", ev_pv)
                sch.use("vector", "atstB%d" % pslot)
                def nrm(e, _s=pslot):
                    return e.tensor_tensor(out=atst[_s][:], in0=PS[4 + _s][:, 0:QT], in1=sbc[:], op=ALU.mult)
                sch.sig("vector", nrm)
                ev_nrm = sch.cnt["vector"]
                sch.setr("psAT%d" % pslot, "vector")
                sch.setr("sbcB", "vector")
                sch.wait("gpsimd", "vector", ev_nrm)
                def sta(e, _s=pslot, _qh=qh, _q0=q0):
                    return e.dma_start(out=attnT_scr[_qh * 128:(_qh + 1) * 128, _q0:_q0 + QT], in_=atst[_s][:])
                sch.dma("gpsimd", sta, "sAt%d" % pslot)
                sch.setr("atstB%d" % pslot, "sAt%d" % pslot)

            pend_fin = None
            pvq = []
            open_banks = {}
            done_head = []

            def pop_pv(nmax=1, min_keep=3):
                n = 0
                while len(pvq) > min_keep and n < nmax:
                    h_, kc_, ps_, fi_, la_, nk_, ev_ = pvq.pop(0)
                    sch.wait("tensor", "vector", ev_)
                    if fi_:
                        sch.use("tensor", "psAT%d" % ps_)
                        sch.use("tensor", "psSUM%d" % ps_)
                    emit_pv_sum2(h_, kc_, ps_, fi_, la_)
                    sch.setr("pp%d_%d" % (ps_, kc_ % 8), "tensor")
                    n += 1
                    if la_:
                        done_head.append(ps_)
                return n

            def emit_pv_sum2(h, kc, pslot, first, last):
                def pv(e, _h=h, _kc=kc, _s=pslot, _st=first, _sp=last):
                    vsl = slice(_kc * HKV * D + _h * 128, _kc * HKV * D + (_h + 1) * 128)
                    psl = slice((_kc % 8) * QT, (_kc % 8 + 1) * QT)
                    e.matmul(PS[4 + _s][:, 0:QT], Vh[:, vsl], ph[_s][:, psl], start=_st, stop=False)
                    e.matmul(PS[4 + _s][:, 0:QT], Vh[:, vsl], pl[_s][:, psl], start=False, stop=False)
                    return e.matmul(PS[4 + _s][:, 0:QT], Vl[:, vsl], ph[_s][:, psl], start=False, stop=_sp)
                if last:
                    sch.sig("tensor", pv)
                else:
                    sch.emit("tensor", pv)
                def osum(e, _kc=kc, _s=pslot, _st=first, _sp=last):
                    psl = slice((_kc % 8) * QT, (_kc % 8 + 1) * QT)
                    e.matmul(PS[6 + _s][0:1, 0:QT], sonesb[:, 0:1], ph[_s][:, psl], start=_st, stop=False)
                    return e.matmul(PS[6 + _s][0:1, 0:QT], sonesb[:, 0:1], pl[_s][:, psl], start=False, stop=_sp)
                if last:
                    sch.sig("tensor", osum)
                else:
                    sch.emit("tensor", osum)

            for qt in range(NQT):
                q0 = qt * QT
                nkc = (q0 + QT) // 128
                sch.use("sync", "kvGate")
                for h in range(HKV):
                    for (scrh, dsth) in ((kT_hi, KTh), (kT_lo, KTl)):
                        def ldk(e, _h=h, _q0=q0, _s=scrh, _d=dsth):
                            return e.dma_start(out=_d[:, _h * S + _q0:_h * S + _q0 + QT],
                                               in_=_s[_h * 128:(_h + 1) * 128, _q0:_q0 + QT])
                        sch.dma("sync", ldk, "kvB")
                for kc in range(4 * qt, 4 * qt + 4):
                    for (scrv, dstv) in ((v_hi, Vh), (v_lo, Vl)):
                        def ldv(e, _kc=kc, _s=scrv, _d=dstv):
                            return e.dma_start(out=_d[:, _kc * HKV * D:(_kc + 1) * HKV * D],
                                               in_=_s[_kc * 128:(_kc + 1) * 128, :])
                        sch.dma("sync", ldv, "kvB")
                ev_kv = sch.cnt["kvB"]

                for qh in range(HQ):
                    h = qh // GRP
                    pslot = qh % 2
                    while done_head:
                        ps_ = done_head.pop(0)
                        ev_pv = sch.cnt["tensor"]
                        sch.wait("vector", "tensor", ev_pv)
                        sch.use("vector", "srecB")
                        sch.sig("vector", lambda e, _s=ps_: e.reciprocal(out=srec[0:1, :], in_=PS[6 + _s][0:1, 0:QT]))
                        ev_rec = sch.cnt["vector"]
                        sch.setr("psSUM%d" % ps_, "vector")
                        finalize((fin_heads.pop(0), ps_, ev_rec, ev_pv, fin_q0.pop(0)))
                    sch.use("sync", "qbufB%d" % pslot)
                    for (scrq, dstq) in ((qT_hi, qbh[pslot]), (qT_lo, qbl[pslot])):
                        def ldq(e, _qh=qh, _q0=q0, _s=scrq, _d=dstq):
                            return e.dma_start(out=_d[:], in_=_s[_qh * 128:(_qh + 1) * 128, _q0:_q0 + QT])
                        sch.dma("sync", ldq, "qB%d" % pslot)
                    ev_q = sch.cnt["qB%d" % pslot]

                    sch.wait("tensor", "kvB", ev_kv)
                    sch.wait("tensor", "qB%d" % pslot, ev_q)
                    for kc in range(nkc):
                        sch.use("tensor", "psSC%d" % (kc % 4))
                        def qk(e, _h=h, _kc=kc, _s=pslot, _b=PS[kc % 4]):
                            sl = slice(_h * S + _kc * 128, _h * S + (_kc + 1) * 128)
                            e.matmul(_b[:, 0:QT], KTh[:, sl], qbh[_s][:], start=True, stop=False)
                            e.matmul(_b[:, 0:QT], KTh[:, sl], qbl[_s][:], start=False, stop=False)
                            return e.matmul(_b[:, 0:QT], KTl[:, sl], qbh[_s][:], start=False, stop=True)
                        sch.sig("tensor", qk)
                        ev_qk = sch.cnt["tensor"]
                        sch.setr("qbufB%d" % pslot, "tensor")
                        if kc == 0 and qh == 0:
                            sch.setr("kvGate", "tensor")
                        pop_pv(1)
                        fslot = kc % 2
                        sch.use("scalar", "pF%d" % fslot)
                        sch.wait("scalar", "tensor", ev_qk)
                        def ex(e, _b=PS[kc % 4], _o=pF[fslot]):
                            return e.activation(out=_o[:], in_=_b[:, 0:QT], func=AF.Exp, bias=0.0, scale=1.0)
                        sch.sig("scalar", ex)
                        ev_exp = sch.cnt["scalar"]
                        sch.setr("psSC%d" % (kc % 4), "scalar")
                        sch.wait("vector", "scalar", ev_exp)
                        sch.use("vector", "pp%d_%d" % (pslot, kc % 8))
                        diag = kc >= 4 * qt
                        delta = kc * 128 - q0
                        def vsplit(e, _o=pF[fslot], _kc=kc, _s=pslot, _diag=diag, _d=delta):
                            psl = slice((_kc % 8) * QT, (_kc % 8 + 1) * QT)
                            if _diag:
                                e.tensor_tensor(out=_o[:], in0=_o[:],
                                                in1=sstair[:, 384 - _d:384 - _d + QT], op=ALU.mult)
                                e.drain()
                            e.tensor_copy(ph[_s][:, psl], _o[:])
                            e.drain()
                            return e.tensor_tensor(out=pl[_s][:, psl], in0=_o[:], in1=ph[_s][:, psl], op=ALU.subtract)
                        sch.sig("vector", vsplit)
                        ev_sub = sch.cnt["vector"]
                        sch.setr("pF%d" % fslot, "vector")
                        pvq.append((h, kc, pslot, kc == 0, kc == nkc - 1, nkc, ev_sub))
                    fin_heads.append(qh)
                    fin_q0.append(q0)
            while pvq:
                pop_pv(1, min_keep=0)
            while done_head:
                ps_ = done_head.pop(0)
                ev_pv = sch.cnt["tensor"]
                sch.wait("vector", "tensor", ev_pv)
                sch.use("vector", "srecB")
                sch.sig("vector", lambda e, _s=ps_: e.reciprocal(out=srec[0:1, :], in_=PS[6 + _s][0:1, 0:QT]))
                ev_rec = sch.cnt["vector"]
                sch.setr("psSUM%d" % ps_, "vector")
                finalize((fin_heads.pop(0), ps_, ev_rec, ev_pv, fin_q0.pop(0)))

        sch.barrier()
        sch.res.clear()

        with ExitStack() as cctx:
            astg = [cctx.enter_context(nc.sbuf_tensor("astgC%d" % i, [128, 4 * CHC], F32)) for i in range(2)]
            ahi = cctx.enter_context(nc.sbuf_tensor("ahiC", [128, NEC * CHC], BF16))
            alo = cctx.enter_context(nc.sbuf_tensor("aloC", [128, NEC * CHC], BF16))
            xTc2 = cctx.enter_context(nc.sbuf_tensor("xTc2C", [128, NEC * CHC], F32))
            wstgc = cctx.enter_context(nc.sbuf_tensor("wstgC", [128, NEC * 128], F32))
            wchi = [cctx.enter_context(nc.sbuf_tensor("wchiC%d" % i, [128, NEC * 128], BF16)) for i in range(2)]
            wclo = [cctx.enter_context(nc.sbuf_tensor("wcloC%d" % i, [128, NEC * 128], BF16)) for i in range(2)]
            sqst = [cctx.enter_context(nc.sbuf_tensor("sqstC%d" % i, [128, CHC], F32)) for i in range(2)]
            m_s = cctx.enter_context(nc.sbuf_tensor("m_sC", [128, CHC], F32))

            def c_loads(c):
                t0 = c * CHC
                ev_g = {}
                for grp_ in range(8):
                    ast = astg[grp_ % 2]
                    sch.use("sync", "astgC%d" % (grp_ % 2))
                    if grp_ < 2:
                        sch.use("sync", "apairC")
                    def lda2(e, _g=grp_, _t0=t0, _a=ast):
                        src = attnT_scr[_g * 512:(_g + 1) * 512, _t0:_t0 + CHC].rearrange("(g p) t -> p g t", p=128)
                        return e.dma_start(out=_a[:].rearrange("p (g t) -> p g t", t=CHC), in_=src)
                    sch.dma("sync", lda2, "ioC%d" % (grp_ % 2))
                    ev_lda = sch.cnt["ioC%d" % (grp_ % 2)]
                    sch.wait("scalar", "ioC%d" % (grp_ % 2), ev_lda)
                    def acast(e, _g=grp_, _a=ast):
                        return e.copy(out=ahi[:, _g * 4 * CHC:(_g + 1) * 4 * CHC], in_=_a[:])
                    sch.sig("scalar", acast)
                    ev_ac = sch.cnt["scalar"]
                    sch.wait("vector", "scalar", ev_ac)
                    def asub(e, _g=grp_, _a=ast):
                        return e.tensor_tensor(out=alo[:, _g * 4 * CHC:(_g + 1) * 4 * CHC], in0=_a[:],
                                               in1=ahi[:, _g * 4 * CHC:(_g + 1) * 4 * CHC], op=ALU.subtract)
                    sch.sig("vector", asub)
                    sch.setr("astgC%d" % (grp_ % 2), "vector")
                    ev_g[grp_] = sch.cnt["vector"]
                return ev_g

            def x_loads(c):
                t0 = c * CHC
                sch.use("sync", "xTc2C")
                for grp_ in range(4):
                    def ldxr(e, _g=grp_, _t0=t0):
                        src = xT[_g * 1024:(_g + 1) * 1024, _t0:_t0 + CHC].rearrange("(g p) t -> p g t", p=128)
                        d = xTc2[:, _g * 8 * CHC:(_g + 1) * 8 * CHC].rearrange("p (g t) -> p g t", t=CHC)
                        return e.dma_start(out=d, in_=src)
                    sch.dma("sync", ldxr, "xrC")
                return sch.cnt["xrC"]

            def w_chain(f, slot, cache):
                if cache:
                    sch.use("sync", "wstgC")
                    def ldw2(e, _f=f):
                        return e.dma_start(out=wstgc[:], in_=wout[_f].rearrange("p a b -> p (a b)"))
                    sch.dma("sync", ldw2, "wC")
                    ev_w = sch.cnt["wC"]
                    sch.wait("scalar", "wC", ev_w)
                    sch.use("scalar", "wpairC%d" % slot)
                    sch.use("scalar", "wpairCs%d" % slot)
                    def wcast2(e, _s=slot):
                        return e.copy(out=wchi[_s][:], in_=wstgc[:])
                    sch.sig("scalar", wcast2)
                    ev_whi = sch.cnt["scalar"]
                    sch.wait("vector", "scalar", ev_whi)
                    def wsub2(e, _s=slot):
                        return e.tensor_tensor(out=wclo[_s][:], in0=wstgc[:], in1=wchi[_s][:], op=ALU.subtract)
                    sch.sig("vector", wsub2)
                    sch.setr("wstgC", "vector")
                    ev_v = sch.cnt["vector"]
                    sch.wait("gpsimd", "vector", ev_v)
                    def stwh(e, _f=f, _s=slot):
                        return e.dma_start(out=wout_hi[_f], in_=wchi[_s][:])
                    sch.dma("gpsimd", stwh, "sWC%d" % slot)
                    def stwl(e, _f=f, _s=slot):
                        return e.dma_start(out=wout_lo[_f], in_=wclo[_s][:])
                    sch.dma("gpsimd", stwl, "sWC%d" % slot)
                    sch.setr("wpairCs%d" % slot, "sWC%d" % slot)
                    return ("vector", ev_v)
                else:
                    sch.use("sync", "wpairC%d" % slot)
                    sch.use("sync", "wpairCs%d" % slot)
                    def ldwh(e, _f=f, _s=slot):
                        return e.dma_start(out=wchi[_s][:], in_=wout_hi[_f])
                    sch.dma("sync", ldwh, "wC%d" % slot)
                    def ldwl(e, _f=f, _s=slot):
                        return e.dma_start(out=wclo[_s][:], in_=wout_lo[_f])
                    sch.dma("sync", ldwl, "wC%d" % slot)
                    ev_w = sch.cnt["wC%d" % slot]
                    sch.setr("wpairCs%d" % slot, "wC%d" % slot)
                    return ("wC%d" % slot, ev_w)

            nxt = c_loads(0)
            nxt_xr = x_loads(0)
            pre_w = None
            for c in range(NCHC):
                t0 = c * CHC
                ev_asplit = nxt
                ev_xr = nxt_xr

                for f in range(NFC_OUT):
                    slot = f % 2
                    if f < 2 and pre_w is not None:
                        wdep = pre_w[f]
                    else:
                        wdep = w_chain(f, slot, c == 0)

                    sch.wait("tensor", wdep[0], wdep[1])
                    sch.use("tensor", "psC%d" % slot)
                    nmm = NEC * 3
                    k = 0
                    for ec in range(NEC):
                        if ec % 4 == 0 and f == 0:
                            sch.wait("tensor", "vector", ev_asplit[ec // 4])
                        for (wo, xo) in ((wchi[slot], ahi), (wchi[slot], alo), (wclo[slot], ahi)):
                            k += 1
                            def mm2(e, _p=PS[slot], _w=wo, _x=xo, _ec=ec, _s=(k == 1), _e=(k == nmm)):
                                return e.matmul(_p[:, 0:CHC], _w[:, _ec * 128:(_ec + 1) * 128],
                                                _x[:, _ec * CHC:(_ec + 1) * CHC], start=_s, stop=_e)
                            if k == nmm:
                                sch.sig("tensor", mm2)
                            else:
                                sch.emit("tensor", mm2)
                    ev_mm = sch.cnt["tensor"]
                    sch.setr("wpairC%d" % slot, "tensor")
                    if f == NFC_OUT - 1:
                        sch.setr("apairC", "tensor")

                    sch.wait("vector", "tensor", ev_mm)
                    if f < 2:
                        sch.wait("vector", "xrC", ev_xr)
                    sch.use("vector", "sqstC%d" % slot)
                    def evc(e, _p=PS[slot], _f=f, _sq=sqst[slot]):
                        xf = xTc2[:, _f * CHC:(_f + 1) * CHC]
                        e.tensor_tensor(out=xf, in0=_p[:, 0:CHC], in1=xf, op=ALU.add)
                        e.drain()
                        return e.tensor_tensor(out=_sq[:], in0=xf, in1=xf, op=ALU.mult)
                    sch.sig("vector", evc)
                    ev_ev = sch.cnt["vector"]
                    sch.setr("psC%d" % slot, "vector")
                    sch.wait("tensor", "vector", ev_ev)
                    def lnm(e, _f=f, _sq=sqst[slot], _s=(f == 0), _e=(f == NFC_OUT - 1)):
                        e.matmul(PS[2][:, 0:CHC], sones[:], xTc2[:, _f * CHC:(_f + 1) * CHC], start=_s, stop=_e)
                        return e.matmul(PS[3][:, 0:CHC], sones[:], _sq[:], start=_s, stop=_e)
                    sch.sig("tensor", lnm)
                    sch.setr("sqstC%d" % slot, "tensor")
                ev_ln = sch.cnt["tensor"]

                sch.wait("vector", "tensor", ev_ln)
                def stats(e):
                    e.tensor_scalar(out=m_s[:], in0=PS[2][:, 0:CHC], scalar1=1.0 / E, scalar2=None, op0=ALU.mult)
                    e.tensor_scalar(out=sqst[0][:], in0=PS[3][:, 0:CHC], scalar1=1.0 / E, scalar2=None, op0=ALU.mult)
                    e.drain()
                    e.tensor_tensor(out=sqst[1][:], in0=m_s[:], in1=m_s[:], op=ALU.mult)
                    e.drain()
                    return e.tensor_tensor(out=sqst[0][:], in0=sqst[0][:], in1=sqst[1][:], op=ALU.subtract)
                sch.sig("vector", stats)
                ev_stats = sch.cnt["vector"]
                sch.wait("scalar", "vector", ev_stats)
                def rstd(e):
                    e.activation(out=sqst[1][:], in_=sqst[0][:], func=AF.Ln, bias=seps[:, 0:1], scale=1.0)
                    e.drain()
                    return e.activation(out=sqst[1][:], in_=sqst[1][:], func=AF.Exp, bias=0.0, scale=-0.5)
                sch.sig("scalar", rstd)
                ev_rstd = sch.cnt["scalar"]

                if c + 1 < NCHC:
                    nxt = c_loads(c + 1)
                    pre_w = [w_chain(0, 0, False), w_chain(1, 1, False)]

                def nsub(e):
                    for _f in range(NFC_OUT):
                        xf = xTc2[:, _f * CHC:(_f + 1) * CHC]
                        e.tensor_tensor(out=xf, in0=xf, in1=m_s[:], op=ALU.subtract)
                    return e.drain()
                sch.sig("vector", nsub)
                sch.wait("vector", "scalar", ev_rstd)
                def nmul(e):
                    for _f in range(NFC_OUT):
                        xf = xTc2[:, _f * CHC:(_f + 1) * CHC]
                        e.tensor_tensor(out=xf, in0=xf, in1=sqst[1][:], op=ALU.mult)
                    return e.drain()
                sch.sig("vector", nmul)
                evn = sch.cnt["vector"]
                sch.wait("gpsimd", "vector", evn)
                for grp_ in range(4):
                    def stx1(e, _g=grp_, _t0=t0):
                        dst = x1T_scr[_g * 1024:(_g + 1) * 1024, _t0:_t0 + CHC].rearrange("(g p) t -> p g t", p=128)
                        s = xTc2[:, _g * 8 * CHC:(_g + 1) * 8 * CHC].rearrange("p (g t) -> p g t", t=CHC)
                        return e.dma_start(out=dst, in_=s)
                    sch.dma("gpsimd", stx1, "sX%d" % (c % 2))
                sch.setr("xTc2C", "sX%d" % (c % 2))
                if c + 1 < NCHC:
                    nxt_xr = x_loads(c + 1)
        sch.barrier()
        sch.res.clear()

        with ExitStack() as dctx:
            x1c2 = [dctx.enter_context(nc.sbuf_tensor("x1cD%d" % i, [128, NEC * CHD], F32)) for i in range(2)]
            abuf = [dctx.enter_context(nc.sbuf_tensor("abufD%d" % i, [128, NEC * 128], F32)) for i in range(2)]
            bbuf = [dctx.enter_context(nc.sbuf_tensor("bbufD%d" % i, [128, NNC * 128], F32)) for i in range(2)]
            zg = dctx.enter_context(nc.sbuf_tensor("zgD", [128, NNC * CHD], F32))
            gx = dctx.enter_context(nc.sbuf_tensor("gxD", [128, NNC * CHD], F32))
            qrT = dctx.enter_context(nc.sbuf_tensor("qrTD", [16, CHD], F32))
            scT = dctx.enter_context(nc.sbuf_tensor("scTD", [64, CHD], F32))
            rtok = dctx.enter_context(nc.sbuf_tensor("rtokD", [128, (CHD // 128) * NEXP], F32))
            rwork = dctx.enter_context(nc.sbuf_tensor("rworkD", [128, NEXP], F32))
            rcmp = dctx.enter_context(nc.sbuf_tensor("rcmpD", [128, NEXP], F32))
            m1n = dctx.enter_context(nc.sbuf_tensor("m1nD", [128, 1], F32))
            thr = dctx.enter_context(nc.sbuf_tensor("thrD", [128, 1], F32))
            eall = dctx.enter_context(nc.sbuf_tensor("eallD", [128, NEXP], F32))
            gsum = dctx.enter_context(nc.sbuf_tensor("gsumD", [128, 1], F32))
            grec = dctx.enter_context(nc.sbuf_tensor("grecD", [128, 1], F32))
            gtm = dctx.enter_context(nc.sbuf_tensor("gtmD", [128, (CHD // 128) * NEXP], F32))
            gT = dctx.enter_context(nc.sbuf_tensor("gTD", [64, CHD], F32))
            sqd = [dctx.enter_context(nc.sbuf_tensor("sqdD%d" % i, [128, CHD], F32)) for i in range(2)]
            m_d = dctx.enter_context(nc.sbuf_tensor("m_dD", [128, CHD], F32))
            r_d = dctx.enter_context(nc.sbuf_tensor("r_dD", [128, CHD], F32))
            t_d = dctx.enter_context(nc.sbuf_tensor("t_dD", [128, CHD], F32))

            for c in range(NCHD):
                t0 = c * CHD
                par = c % 2
                x1c = x1c2[par]
                sch.use("sync", "x1cDt%d" % par)
                sch.use("sync", "x1cDv%d" % par)
                for grp_ in range(4):
                    def ldd(e, _g=grp_, _t0=t0, _x=x1c):
                        src = x1T_scr[_g * 1024:(_g + 1) * 1024, _t0:_t0 + CHD].rearrange("(g p) t -> p g t", p=128)
                        d = _x[:, _g * 8 * CHD:(_g + 1) * 8 * CHD].rearrange("p (g t) -> p g t", t=CHD)
                        return e.dma_start(out=d, in_=src)
                    sch.dma("sync", ldd, "xD%d" % par)
                ev_xd = sch.cnt["xD%d" % par]

                sch.wait("tensor", "xD%d" % par, ev_xd)
                sch.wait("tensor", "cN", ld_consts)
                sch.use("tensor", "psD0")
                for ec in range(NEC):
                    def rq(e, _ec=ec, _x=x1c, _s=(ec == 0), _e=(ec == NEC - 1)):
                        return e.matmul(PS[0][0:16, 0:CHD], swq[:, _ec * 16:(_ec + 1) * 16],
                                        _x[:, _ec * CHD:(_ec + 1) * CHD], start=_s, stop=_e)
                    if ec == NEC - 1:
                        sch.sig("tensor", rq)
                    else:
                        sch.emit("tensor", rq)
                ev_rq = sch.cnt["tensor"]
                sch.wait("vector", "tensor", ev_rq)
                sch.sig("vector", lambda e: e.tensor_copy(qrT[:], PS[0][0:16, 0:CHD]))
                ev_qr = sch.cnt["vector"]
                sch.setr("psD0", "vector")
                sch.wait("tensor", "vector", ev_qr)
                sch.use("tensor", "psD1")
                sch.sig("tensor", lambda e: e.matmul(PS[1][0:64, 0:CHD], skeys[:], qrT[:], start=True, stop=True))
                ev_s64 = sch.cnt["tensor"]
                sch.wait("vector", "tensor", ev_s64)
                sch.sig("vector", lambda e: e.tensor_copy(scT[:], PS[1][0:64, 0:CHD]))
                ev_sce = sch.cnt["vector"]
                sch.setr("psD1", "vector")
                sch.wait("tensor", "vector", ev_sce)
                for tb in range(CHD // 128):
                    bank = tb % 2
                    sch.use("tensor", "psD%d" % bank)
                    def trr(e, _tb=tb, _b=bank):
                        return e.transpose(PS[_b][:, 0:64], scT[:, _tb * 128:(_tb + 1) * 128], sid[0:64, 0:64])
                    sch.sig("tensor", trr)
                    evt = sch.cnt["tensor"]
                    sch.wait("vector", "tensor", evt)
                    def trre(e, _tb=tb, _b=bank):
                        return e.tensor_copy(rtok[:, _tb * NEXP:(_tb + 1) * NEXP], PS[_b][:, 0:64])
                    sch.sig("vector", trre)
                    sch.setr("psD%d" % bank, "vector")
                ev_rtok = sch.cnt["vector"]

                ev_zmm = {}
                for nc_ in range(NNC):
                    gslot = (c * NNC + nc_) % 2
                    sch.use("sync", "abufD%d" % gslot)
                    def lda(e, _n=nc_, _s=gslot):
                        return e.dma_start(out=abuf[_s][:], in_=wa[_n].rearrange("p a b -> p (a b)"))
                    sch.dma("sync", lda, "aD%d" % gslot)
                    ev_wa = sch.cnt["aD%d" % gslot]
                    sch.wait("tensor", "aD%d" % gslot, ev_wa)
                    sch.use("tensor", "psD%d" % nc_)
                    for ec in range(NEC):
                        def zmm(e, _p=PS[nc_], _w=abuf[gslot], _ec=ec, _x=x1c, _s=(ec == 0), _e=(ec == NEC - 1)):
                            return e.matmul(_p[:, 0:CHD], _w[:, _ec * 128:(_ec + 1) * 128],
                                            _x[:, _ec * CHD:(_ec + 1) * CHD], start=_s, stop=_e)
                        if ec == NEC - 1:
                            sch.sig("tensor", zmm)
                        else:
                            sch.emit("tensor", zmm)
                    ev_zmm[nc_] = sch.cnt["tensor"]
                    sch.setr("abufD%d" % gslot, "tensor")

                for tb in range(CHD // 128):
                    rt = rtok[:, tb * NEXP:(tb + 1) * NEXP]
                    def tk(e, _rt=rt):
                        e.tensor_copy(rwork[:], _rt)
                        e.tensor_reduce(out=m1n[:], in_=_rt, axis=AX.X, op=ALU.max, negate=True)
                        e.drain()
                        for i in range(TOPK):
                            e.tensor_reduce(out=thr[:], in_=rwork[:], axis=AX.X, op=ALU.max)
                            e.drain()
                            if i < TOPK - 1:
                                e.tensor_scalar(out=rcmp[:], in0=rwork[:], scalar1=thr[:], scalar2=None, op0=ALU.is_ge)
                                e.drain()
                                e.scalar_tensor_tensor(out=rwork[:], in0=rcmp[:], scalar=-3.0e38, in1=rwork[:],
                                                       op0=ALU.mult, op1=ALU.add)
                                e.drain()
                        return e.nop()
                    sch.sig("vector", tk)
                    ev_tk = sch.cnt["vector"]
                    sch.wait("scalar", "vector", ev_tk)
                    def gexp(e, _rt=rt):
                        return e.activation(out=eall[:], in_=_rt, func=AF.Exp, bias=m1n[:, 0:1], scale=1.0)
                    sch.sig("scalar", gexp)
                    ev_ge = sch.cnt["scalar"]
                    sch.wait("vector", "scalar", ev_ge)
                    def gfin(e, _rt=rt, _tb=tb):
                        e.tensor_scalar(out=rcmp[:], in0=_rt, scalar1=thr[:], scalar2=None, op0=ALU.is_ge)
                        e.drain()
                        e.scalar_tensor_tensor(out=eall[:], in0=eall[:], scalar=1.0, in1=rcmp[:],
                                               op0=ALU.mult, op1=ALU.mult, accum_out=gsum[:])
                        e.drain()
                        e.reciprocal(out=grec[:], in_=gsum[:])
                        e.drain()
                        return e.tensor_scalar(out=gtm[:, _tb * NEXP:(_tb + 1) * NEXP], in0=eall[:],
                                               scalar1=grec[:], scalar2=2.0, op0=ALU.mult, op1=ALU.mult)
                    sch.sig("vector", gfin)
                ev_gates = sch.cnt["vector"]

                sch.wait("tensor", "vector", ev_gates)
                for tb in range(CHD // 128):
                    bank = 4 + tb % 2
                    sch.use("tensor", "psD%d" % bank)
                    def gtr(e, _tb=tb, _b=bank):
                        return e.transpose(PS[_b][0:64, 0:128], gtm[:, _tb * NEXP:(_tb + 1) * NEXP], sid[:])
                    sch.sig("tensor", gtr)
                    evg = sch.cnt["tensor"]
                    sch.wait("vector", "tensor", evg)
                    def gtre(e, _tb=tb, _b=bank):
                        return e.tensor_copy(gT[:, _tb * 128:(_tb + 1) * 128], PS[_b][0:64, 0:128])
                    sch.sig("vector", gtre)
                    sch.setr("psD%d" % bank, "vector")
                ev_gT = sch.cnt["vector"]
                sch.wait("tensor", "vector", ev_gT)
                for nc_ in range(NNC):
                    bank = 4 + (nc_ % 2)
                    sch.use("tensor", "psD%d" % bank)
                    def gem(e, _n=nc_, _b=bank):
                        return e.matmul(PS[_b][:, 0:CHD], semat[:, _n * 128:(_n + 1) * 128], gT[:],
                                        start=True, stop=True)
                    sch.sig("tensor", gem)
                    evgm = sch.cnt["tensor"]
                    sch.wait("scalar", "tensor", evgm)
                    if nc_ == 0:
                        sch.use("scalar", "gxD")
                    def geme(e, _n=nc_, _b=bank):
                        return e.copy(out=gx[:, _n * CHD:(_n + 1) * CHD], in_=PS[_b][:, 0:CHD])
                    sch.sig("scalar", geme)
                    sch.setr("psD%d" % bank, "scalar")
                ev_gx = sch.cnt["scalar"]

                for nc_ in range(NNC):
                    sch.wait("vector", "tensor", ev_zmm[nc_])
                    sch.wait("vector", "scalar", ev_gx)
                    if nc_ == 0:
                        sch.use("vector", "zgD")
                    def zev(e, _p=PS[nc_], _n=nc_):
                        return e.scalar_tensor_tensor(out=zg[:, _n * CHD:(_n + 1) * CHD], in0=_p[:, 0:CHD],
                                                      scalar=1.0, in1=gx[:, _n * CHD:(_n + 1) * CHD],
                                                      op0=ALU.mult, op1=ALU.mult)
                    sch.sig("vector", zev)
                    sch.setr("psD%d" % nc_, "vector")
                ev_zg = sch.cnt["vector"]
                sch.setr("gxD", "vector")

                for f in range(NFC_OUT):
                    slot = f % 2
                    bank = 4 + slot
                    sch.use("sync", "bbufD%d" % slot)
                    def ldb(e, _f=f, _s=slot):
                        return e.dma_start(out=bbuf[_s][:], in_=wb[_f].rearrange("p a b -> p (a b)"))
                    sch.dma("sync", ldb, "bD%d" % slot)
                    ev_wb = sch.cnt["bD%d" % slot]
                    sch.wait("tensor", "bD%d" % slot, ev_wb)
                    sch.wait("tensor", "vector", ev_zg)
                    sch.use("tensor", "psD%d" % bank)
                    for n_ in range(NNC):
                        def bmm(e, _p=PS[bank], _w=bbuf[slot], _n=n_, _s=(n_ == 0), _e=(n_ == NNC - 1)):
                            return e.matmul(_p[:, 0:CHD], _w[:, _n * 128:(_n + 1) * 128],
                                            zg[:, _n * CHD:(_n + 1) * CHD], start=_s, stop=_e)
                        if n_ == NNC - 1:
                            sch.sig("tensor", bmm)
                        else:
                            sch.emit("tensor", bmm)
                    ev_b = sch.cnt["tensor"]
                    sch.setr("bbufD%d" % slot, "tensor")
                    if f == NFC_OUT - 1:
                        sch.setr("zgD", "tensor")
                    sch.wait("vector", "tensor", ev_b)
                    sch.use("vector", "sqdD%d" % slot)
                    def evo(e, _p=PS[bank], _f=f, _sq=sqd[slot], _x=x1c):
                        xf = _x[:, _f * CHD:(_f + 1) * CHD]
                        e.tensor_tensor(out=xf, in0=_p[:, 0:CHD], in1=xf, op=ALU.add)
                        e.drain()
                        return e.tensor_tensor(out=_sq[:], in0=xf, in1=xf, op=ALU.mult)
                    sch.sig("vector", evo)
                    ev_eo = sch.cnt["vector"]
                    sch.setr("psD%d" % bank, "vector")
                    sch.wait("tensor", "vector", ev_eo)
                    def lnm2(e, _f=f, _sq=sqd[slot], _x=x1c, _s=(f == 0), _e=(f == NFC_OUT - 1)):
                        e.matmul(PS[6][:, 0:CHD], sones[:], _x[:, _f * CHD:(_f + 1) * CHD], start=_s, stop=_e)
                        return e.matmul(PS[7][:, 0:CHD], sones[:], _sq[:], start=_s, stop=_e)
                    sch.sig("tensor", lnm2)
                    sch.setr("sqdD%d" % slot, "tensor")
                ev_ln2 = sch.cnt["tensor"]
                sch.setr("x1cDt%d" % par, "tensor")

                sch.wait("vector", "tensor", ev_ln2)
                def stats2(e):
                    e.tensor_scalar(out=m_d[:], in0=PS[6][:, 0:CHD], scalar1=1.0 / E, scalar2=None, op0=ALU.mult)
                    e.tensor_scalar(out=t_d[:], in0=PS[7][:, 0:CHD], scalar1=1.0 / E, scalar2=None, op0=ALU.mult)
                    e.drain()
                    e.tensor_tensor(out=r_d[:], in0=m_d[:], in1=m_d[:], op=ALU.mult)
                    e.drain()
                    return e.tensor_tensor(out=t_d[:], in0=t_d[:], in1=r_d[:], op=ALU.subtract)
                sch.sig("vector", stats2)
                ev_st2 = sch.cnt["vector"]
                sch.wait("scalar", "vector", ev_st2)
                def rstd2(e):
                    e.activation(out=r_d[:], in_=t_d[:], func=AF.Ln, bias=seps[:, 0:1], scale=1.0)
                    e.drain()
                    return e.activation(out=r_d[:], in_=r_d[:], func=AF.Exp, bias=0.0, scale=-0.5)
                sch.sig("scalar", rstd2)
                ev_rs2 = sch.cnt["scalar"]
                def nsub2(e, _x=x1c):
                    for _f in range(NFC_OUT):
                        xf = _x[:, _f * CHD:(_f + 1) * CHD]
                        e.tensor_tensor(out=xf, in0=xf, in1=m_d[:], op=ALU.subtract)
                    return e.drain()
                sch.sig("vector", nsub2)
                sch.wait("vector", "scalar", ev_rs2)
                def nmul2(e, _x=x1c):
                    for _f in range(NFC_OUT):
                        xf = _x[:, _f * CHD:(_f + 1) * CHD]
                        e.tensor_tensor(out=xf, in0=xf, in1=r_d[:], op=ALU.mult)
                    return e.drain()
                sch.sig("vector", nmul2)
                evn2 = sch.cnt["vector"]
                sch.wait("gpsimd", "vector", evn2)
                for grp_ in range(4):
                    def sto(e, _g=grp_, _t0=t0, _x=x1c):
                        dst = outT[_g * 1024:(_g + 1) * 1024, _t0:_t0 + CHD].rearrange("(g p) t -> p g t", p=128)
                        s = _x[:, _g * 8 * CHD:(_g + 1) * 8 * CHD].rearrange("p (g t) -> p g t", t=CHD)
                        return e.dma_start(out=dst, in_=s)
                    sch.dma("gpsimd", sto, "sO%d" % par)
                sch.setr("x1cDv%d" % par, "sO%d" % par)

            sch.ops["gpsimd"].append(("wait", "sO0", sch.cnt["sO0"]))
            sch.ops["gpsimd"].append(("wait", "sO1", sch.cnt["sO1"]))

        sems = {name: ctx.enter_context(nc.semaphore("sem_" + name)) for name in sch.cnt}
        with nc.Block() as block:
            sch.playback(block, sems)
    return nc



def make_consts():
    import jax
    import jax.numpy as jnp
    with jax.default_device(jax.devices("cpu")[0]):
        inv_freq32 = 1.0 / (10000.0 ** (jnp.arange(0, D, 2, dtype=jnp.float32) / D))
        ang32 = jnp.arange(S, dtype=jnp.float32)[:, None] * inv_freq32[None, :]
        cos32 = np.asarray(jnp.cos(ang32)).astype(np.float64).T
        sin32 = np.asarray(jnp.sin(ang32)).astype(np.float64).T

    def pack(c, s):
        out = np.zeros((2, 2, 128, S), dtype=np.float32)
        chi = c.astype(np.float32); shi = s.astype(np.float32)
        clo = (c - chi.astype(np.float64)).astype(np.float32)
        slo = (s - shi.astype(np.float64)).astype(np.float32)
        for half in range(2):
            out[0, 0, half * 64:(half + 1) * 64] = chi
            out[0, 1, half * 64:(half + 1) * 64] = shi
            out[1, 0, half * 64:(half + 1) * 64] = clo
            out[1, 1, half * 64:(half + 1) * 64] = slo
        return out

    ropeQ = pack(cos32 * np.float64(SCALE_QK), sin32 * np.float64(SCALE_QK))
    ropeK = pack(cos32, sin32)
    stair = (np.arange(896)[None, :] >= (np.arange(128)[:, None] + 384)).astype(np.float32)
    emat = np.zeros((64, BOT), dtype=np.float32)
    for n in range(BOT):
        emat[n // RANK, n] = 1.0
    ident = np.eye(128, dtype=np.float32)
    return ropeQ, ropeK, stair, emat, ident


def qk_row_perm():
    perm = []
    for sec_base, nh in ((0, HQ), (4096, HKV)):
        for j in range(nh // 2):
            lo = []
            hi = []
            for hh in (2 * j, 2 * j + 1):
                base = sec_base + hh * 128
                lo.extend(range(base, base + 64))
                hi.extend(range(base + 64, base + 128))
            perm.extend(lo)
            perm.extend(hi)
    perm.extend(range(5120, 6144))
    return np.asarray(perm)


def prep_weights(w_in, w_out, A_w, B_w, Wq_w, keys):
    w_in = np.asarray(w_in)[qk_row_perm()]
    wqkv = np.ascontiguousarray(w_in.T.reshape(NEC, 128, NFC_QKV, 128).transpose(2, 1, 0, 3))
    woutb = np.ascontiguousarray(w_out.T.reshape(NEC, 128, NFC_OUT, 128).transpose(2, 1, 0, 3))
    wab = np.ascontiguousarray(A_w.T.reshape(NEC, 128, NNC, 128).transpose(2, 1, 0, 3))
    wbb = np.ascontiguousarray(B_w.T.reshape(NNC, 128, NFC_OUT, 128).transpose(2, 1, 0, 3))
    wqTb = np.ascontiguousarray(Wq_w.T)
    keysTb = np.ascontiguousarray(keys.T)
    return wqkv, woutb, wab, wbb, wqTb, keysTb


def kernel(L0_attn_x, L0_attn_w_in, L0_attn_w_out, L0_ffn_A_w, L0_ffn_B_w, L0_ffn_Wq_w, L0_ffn_keys,
           dbg=False, n_cores=8):
    x = np.asarray(L0_attn_x)
    wqkv, woutb, wab, wbb, wqTb, keysTb = prep_weights(
        np.asarray(L0_attn_w_in), np.asarray(L0_attn_w_out), np.asarray(L0_ffn_A_w),
        np.asarray(L0_ffn_B_w), np.asarray(L0_ffn_Wq_w), np.asarray(L0_ffn_keys))
    ropeQ, ropeK, stair, emat, ident = make_consts()
    nc = build(dbg=dbg)
    in_maps = []
    for b in range(n_cores):
        in_maps.append({
            "xT": np.ascontiguousarray(x[b].T),
            "wqkv": wqkv, "wout": woutb, "wa": wab, "wb": wbb,
            "wqT": wqTb, "keysT": keysTb,
            "ropeQ": ropeQ, "ropeK": ropeK, "stair": stair, "emat": emat, "ident": ident,
        })
    res = run_bass_kernel_spmd(nc, in_maps, core_ids=list(range(n_cores)))
    if dbg:
        return res
    out = np.stack([np.asarray(r["outT"]).T for r in res.results], axis=0)
    return np.ascontiguousarray(out.astype(np.float32))


# version 16
# speedup vs baseline: 1.1403x; 1.1403x over previous
import numpy as np
from contextlib import ExitStack

import concourse.bass as bass
import concourse.mybir as mybir
from concourse.bass_utils import run_bass_kernel_spmd
from concourse.alu_op_type import AluOpType as ALU

F32 = mybir.dt.float32
BF16 = mybir.dt.bfloat16
AF = mybir.ActivationFunctionType
AX = mybir.AxisListType

S, E, FQKV = 2048, 4096, 6144
HQ, HKV, D, GRP = 32, 8, 128, 4
NEC = 32
NFC_QKV, NFC_OUT = 48, 32
NEXP, TOPK, RANK, BOT = 64, 16, 8, 512
NNC = 4
SCALE_QK = 1.0 / np.sqrt(128.0)
EPS = 1e-5
CHA = 512
NCHA = S // CHA
QT = 512
NQT = S // QT
CHC = 512
NCHC = S // CHC
CHD = 512
NCHD = S // CHD

ENGS = ("sync", "tensor", "vector", "scalar", "gpsimd")


class Sch:

    def __init__(self):
        self.ops = {e: [] for e in ENGS}
        self.cnt = {}
        self.res = {}
        for e in ENGS:
            self.cnt[e] = 0

    def emit(self, eng, fn):
        self.ops[eng].append(("op", fn))

    def sig(self, eng, fn, sem=None, n=1):
        sem = sem or eng
        self.cnt.setdefault(sem, 0)
        self.ops[eng].append(("sig", fn, sem, n))
        self.cnt[sem] += n
        return self.cnt[sem]

    def dma(self, eng, fn, sem):
        return self.sig(eng, fn, sem, 16)

    def wait(self, eng, sem, val):
        if val > 0 and sem != eng:
            self.ops[eng].append(("wait", sem, val))

    def use(self, eng, resname):
        ev = self.res.get(resname)
        if ev is not None:
            self.wait(eng, ev[0], ev[1])

    def setr(self, resname, sem):
        self.res[resname] = (sem, self.cnt[sem])

    def barrier(self):
        snap = dict(self.cnt)
        for e in ENGS:
            for o, v in snap.items():
                if o != e and v > 0:
                    self.wait(e, o, v)

    def playback(self, block, sems):
        for engname in ENGS:
            ops = self.ops[engname]

            def body(eng, _ops=ops):
                for rec in _ops:
                    if rec[0] == "op":
                        rec[1](eng)
                    elif rec[0] == "sig":
                        rec[1](eng).then_inc(sems[rec[2]], rec[3])
                    else:
                        eng.wait_ge(sems[rec[1]], rec[2])

            getattr(block, engname)(body)


def build(dbg=False):
    nc = bass.Bass("TRN2", target_bir_lowering=False)
    xT = nc.dram_tensor("xT", [E, S], F32, kind="ExternalInput").ap()
    wqkv = nc.dram_tensor("wqkv", [NFC_QKV, 128, NEC, 128], F32, kind="ExternalInput").ap()
    wout = nc.dram_tensor("wout", [NFC_OUT, 128, NEC, 128], F32, kind="ExternalInput").ap()
    wa = nc.dram_tensor("wa", [NNC, 128, NEC, 128], F32, kind="ExternalInput").ap()
    wb = nc.dram_tensor("wb", [NFC_OUT, 128, NNC, 128], F32, kind="ExternalInput").ap()
    wqT = nc.dram_tensor("wqT", [E, 16], F32, kind="ExternalInput").ap()
    keysT = nc.dram_tensor("keysT", [16, NEXP], F32, kind="ExternalInput").ap()
    ropeQ = nc.dram_tensor("ropeQ", [2, 2, 128, S], F32, kind="ExternalInput").ap()
    ropeK = nc.dram_tensor("ropeK", [2, 2, 128, S], F32, kind="ExternalInput").ap()
    stair = nc.dram_tensor("stair", [128, 896], F32, kind="ExternalInput").ap()
    emat = nc.dram_tensor("emat", [64, BOT], F32, kind="ExternalInput").ap()
    ident = nc.dram_tensor("ident", [128, 128], F32, kind="ExternalInput").ap()
    outT = nc.dram_tensor("outT", [E, S], F32, kind="ExternalOutput").ap()
    kscr = "ExternalOutput" if dbg else "Internal"
    qT_hi = nc.dram_tensor("qT_hi", [E, S], BF16, kind=kscr).ap()
    qT_lo = nc.dram_tensor("qT_lo", [E, S], BF16, kind=kscr).ap()
    kT_hi = nc.dram_tensor("kT_hi", [HKV * D, S], BF16, kind=kscr).ap()
    kT_lo = nc.dram_tensor("kT_lo", [HKV * D, S], BF16, kind=kscr).ap()
    v_hi = nc.dram_tensor("v_hi", [S, HKV * D], BF16, kind=kscr).ap()
    v_lo = nc.dram_tensor("v_lo", [S, HKV * D], BF16, kind=kscr).ap()
    attnT_scr = nc.dram_tensor("attnT_scr", [E, S], F32, kind=kscr).ap()
    wqkv_hi = nc.dram_tensor("wqkv_hi", [NFC_QKV, 128, NEC * 128], BF16, kind="Internal").ap()
    wqkv_lo = nc.dram_tensor("wqkv_lo", [NFC_QKV, 128, NEC * 128], BF16, kind="Internal").ap()
    wout_hi = nc.dram_tensor("wout_hi", [NFC_OUT, 128, NEC * 128], BF16, kind="Internal").ap()
    wout_lo = nc.dram_tensor("wout_lo", [NFC_OUT, 128, NEC * 128], BF16, kind="Internal").ap()
    x1T_scr = nc.dram_tensor("x1T_scr", [E, S], F32, kind=kscr).ap()

    sch = Sch()
    with ExitStack() as ctx:
        sid = ctx.enter_context(nc.sbuf_tensor("sid", [128, 128], F32))
        sstair = ctx.enter_context(nc.sbuf_tensor("sstair", [128, 896], F32))
        semat = ctx.enter_context(nc.sbuf_tensor("semat", [64, BOT], F32))
        skeys = ctx.enter_context(nc.sbuf_tensor("skeys", [16, NEXP], F32))
        swq = ctx.enter_context(nc.sbuf_tensor("swq", [128, NEC * 16], F32))
        sones = ctx.enter_context(nc.sbuf_tensor("sones", [128, 128], F32))
        seps = ctx.enter_context(nc.sbuf_tensor("seps", [128, 1], F32))
        sonesb = ctx.enter_context(nc.sbuf_tensor("sonesb", [128, 2], BF16))
        PS = [ctx.enter_context(nc.psum_tensor("ps%d" % i, [128, 512], F32)) for i in range(8)]

        sch.dma("sync", lambda e: e.dma_start(out=sid[:], in_=ident[:]), "cN")
        sch.dma("sync", lambda e: e.dma_start(out=sstair[:], in_=stair[:]), "cN")
        sch.dma("sync", lambda e: e.dma_start(out=semat[:], in_=emat[:]), "cN")
        sch.dma("sync", lambda e: e.dma_start(out=skeys[:], in_=keysT[:]), "cN")
        sch.dma("sync", lambda e: e.dma_start(
            out=swq[:].rearrange("p (g r) -> p g r", r=16),
            in_=wqT[:, :].rearrange("(g p) r -> p g r", p=128)), "cN")
        ld_consts = sch.cnt["cN"]
        sch.emit("vector", lambda e: e.memset(sones[:], 1.0))
        sch.emit("vector", lambda e: e.memset(sonesb[:], 1.0))
        sch.sig("vector", lambda e: e.memset(seps[:], EPS))
        ve_ones = sch.cnt["vector"]

        with ExitStack() as actx:
            xstg = [actx.enter_context(nc.sbuf_tensor("xstgA%d" % i, [128, 4 * CHA], F32)) for i in range(2)]
            xhi = actx.enter_context(nc.sbuf_tensor("xhiA", [128, NEC * CHA], BF16))
            xlo = actx.enter_context(nc.sbuf_tensor("xloA", [128, NEC * CHA], BF16))
            wstg = [actx.enter_context(nc.sbuf_tensor("wstgA%d" % i, [128, NEC * 128], F32)) for i in range(2)]
            whi = [actx.enter_context(nc.sbuf_tensor("whiA%d" % i, [128, NEC * 128], BF16)) for i in range(2)]
            wlo = [actx.enter_context(nc.sbuf_tensor("wloA%d" % i, [128, NEC * 128], BF16)) for i in range(2)]
            rQ = [[actx.enter_context(nc.sbuf_tensor("rQ%d%d" % (a, b), [128, CHA], F32)) for b in range(2)] for a in range(2)]
            rK = [[actx.enter_context(nc.sbuf_tensor("rK%d%d" % (a, b), [128, CHA], F32)) for b in range(2)] for a in range(2)]
            qkst = [actx.enter_context(nc.sbuf_tensor("qkstA%d" % i, [128, CHA], F32)) for i in range(2)]
            vst = [actx.enter_context(nc.sbuf_tensor("vstA%d" % i, [128, CHA], F32)) for i in range(2)]
            vtk = [actx.enter_context(nc.sbuf_tensor("vtkA%d" % i, [128, CHA], F32)) for i in range(2)]
            tmpA = [actx.enter_context(nc.sbuf_tensor("tmpA%d" % i, [128, CHA], F32)) for i in range(8)]
            qkh = [actx.enter_context(nc.sbuf_tensor("qkhA%d" % i, [128, CHA], BF16)) for i in range(2)]
            qkl = [actx.enter_context(nc.sbuf_tensor("qklA%d" % i, [128, CHA], BF16)) for i in range(2)]
            vhs = [actx.enter_context(nc.sbuf_tensor("vhsA%d" % i, [128, CHA], BF16)) for i in range(2)]
            vls = [actx.enter_context(nc.sbuf_tensor("vlsA%d" % i, [128, CHA], BF16)) for i in range(2)]

            def emit_rope(pend):
                Lb, Hb, rt, j, scr_hi, scr_lo, ev_mm, ev_r, t0p = pend
                sch.wait("vector", "tensor", ev_mm)
                sch.wait("vector", "rA", ev_r)
                sch.use("vector", "qkstA0")
                sch.use("vector", "qkstA1")
                def rope(e, _L=Lb, _H=Hb, _rt=rt):
                    L = _L[:, 0:CHA]; H = _H[:, 0:CHA]
                    dlo = qkst[0][:]; dhi = qkst[1][:]
                    t = [x[:] for x in tmpA]
                    e.tensor_tensor(out=t[0], in0=L, in1=_rt[0][0][:], op=ALU.mult)
                    e.tensor_tensor(out=t[4], in0=H, in1=_rt[0][0][:], op=ALU.mult)
                    e.tensor_tensor(out=t[1], in0=H, in1=_rt[0][1][:], op=ALU.mult)
                    e.tensor_tensor(out=t[5], in0=L, in1=_rt[0][1][:], op=ALU.mult)
                    e.tensor_tensor(out=t[2], in0=L, in1=_rt[1][0][:], op=ALU.mult)
                    e.tensor_tensor(out=t[6], in0=H, in1=_rt[1][0][:], op=ALU.mult)
                    e.tensor_tensor(out=t[3], in0=H, in1=_rt[1][1][:], op=ALU.mult)
                    e.tensor_tensor(out=t[7], in0=L, in1=_rt[1][1][:], op=ALU.mult)
                    e.drain()
                    e.tensor_tensor(out=dlo, in0=t[0], in1=t[1], op=ALU.subtract)
                    e.tensor_tensor(out=dhi, in0=t[4], in1=t[5], op=ALU.add)
                    e.drain()
                    e.tensor_tensor(out=t[1], in0=t[2], in1=t[3], op=ALU.subtract)
                    e.tensor_tensor(out=t[5], in0=t[6], in1=t[7], op=ALU.add)
                    e.drain()
                    e.tensor_tensor(out=dlo, in0=dlo, in1=t[1], op=ALU.add)
                    return e.tensor_tensor(out=dhi, in0=dhi, in1=t[5], op=ALU.add)
                sch.sig("vector", rope)
                sch.setr("psA%d" % PS.index(Lb), "vector")
                sch.setr("psA%d" % PS.index(Hb), "vector")
                ev_rope = sch.cnt["vector"]
                for half in range(2):
                    sch.wait("scalar", "vector", ev_rope)
                    sch.use("scalar", "qkhA%d" % half)
                    def qcast(e, _h=half):
                        return e.copy(out=qkh[_h][:], in_=qkst[_h][:])
                    sch.sig("scalar", qcast)
                    ev_qc = sch.cnt["scalar"]
                    sch.wait("vector", "scalar", ev_qc)
                    def qsub(e, _h=half):
                        e.drain()
                        return e.tensor_tensor(out=qkl[_h][:], in0=qkst[_h][:], in1=qkh[_h][:], op=ALU.subtract)
                    sch.sig("vector", qsub)
                    ev_qs = sch.cnt["vector"]
                    sch.setr("qkstA%d" % half, "vector")
                    sch.wait("gpsimd", "vector", ev_qs)
                    for (buf, scrp) in ((qkh, scr_hi), (qkl, scr_lo)):
                        for hh in range(2):
                            def stq(e, _j=j, _half=half, _hh=hh, _b=buf, _s=scrp, _t0=t0p):
                                r0 = (_j * 2 + _hh) * 128 + _half * 64
                                return e.dma_start(out=_s[r0:r0 + 64, _t0:_t0 + CHA],
                                                   in_=_b[_half][_hh * 64:(_hh + 1) * 64, :])
                            sch.dma("gpsimd", stq, "sQK%d" % half)
                    sch.setr("qkhA%d" % half, "sQK%d" % half)

            for c in range(NCHA):
                t0 = c * CHA
                pending = None
                ev_xsplit_g = {}
                for grp_ in range(8):
                    xs = xstg[grp_ % 2]
                    sch.use("sync", "xstgA%d" % (grp_ % 2))
                    if grp_ < 2:
                        sch.use("sync", "xpairA")
                    def ldx(e, _g=grp_, _t0=t0, _xs=xs):
                        src = xT[_g * 512:(_g + 1) * 512, _t0:_t0 + CHA].rearrange("(g p) t -> p g t", p=128)
                        return e.dma_start(out=_xs[:].rearrange("p (g t) -> p g t", t=CHA), in_=src)
                    sch.dma("sync", ldx, "xA%d" % (grp_ % 2))
                    ev_ldx = sch.cnt["xA%d" % (grp_ % 2)]
                    sch.wait("scalar", "xA%d" % (grp_ % 2), ev_ldx)
                    def xcast(e, _g=grp_, _xs=xs):
                        return e.copy(out=xhi[:, _g * 4 * CHA:(_g + 1) * 4 * CHA], in_=_xs[:])
                    sch.sig("scalar", xcast)
                    ev_xc = sch.cnt["scalar"]
                    sch.wait("vector", "scalar", ev_xc)
                    def xsub(e, _g=grp_, _xs=xs):
                        return e.tensor_tensor(out=xlo[:, _g * 4 * CHA:(_g + 1) * 4 * CHA], in0=_xs[:],
                                               in1=xhi[:, _g * 4 * CHA:(_g + 1) * 4 * CHA], op=ALU.subtract)
                    sch.sig("vector", xsub)
                    sch.setr("xstgA%d" % (grp_ % 2), "vector")
                    ev_xsplit_g[grp_] = sch.cnt["vector"]
                ev_xsplit = sch.cnt["vector"]

                for hl in range(2):
                    for cs in range(2):
                        def ldr(e, _h=hl, _c=cs, _t0=t0):
                            return e.dma_start(out=rQ[_h][_c][:], in_=ropeQ[_h, _c][:, _t0:_t0 + CHA])
                        sch.dma("sync", ldr, "rA")
                        def ldr2(e, _h=hl, _c=cs, _t0=t0):
                            return e.dma_start(out=rK[_h][_c][:], in_=ropeK[_h, _c][:, _t0:_t0 + CHA])
                        sch.dma("sync", ldr2, "rA")
                ev_r = sch.cnt["rA"]

                for f in range(NFC_QKV):
                    g = c * NFC_QKV + f
                    wslot = g % 2
                    bank = g % 4
                    if c == 0:
                        sch.use("sync", "wstgA%d" % wslot)
                        def ldw(e, _f=f, _w=wstg[wslot]):
                            return e.dma_start(out=_w[:], in_=wqkv[_f].rearrange("p a b -> p (a b)"))
                        sch.dma("sync", ldw, "wA%d" % wslot)
                        ev_w = sch.cnt["wA%d" % wslot]
                        sch.wait("scalar", "wA%d" % wslot, ev_w)
                        sch.use("scalar", "wpairA%d" % wslot)
                        sch.use("scalar", "wpairAs%d" % wslot)
                        def wcast(e, _s=wslot):
                            return e.copy(out=whi[_s][:], in_=wstg[_s][:])
                        sch.sig("scalar", wcast)
                        ev_whi = sch.cnt["scalar"]
                        sch.wait("vector", "scalar", ev_whi)
                        def wsub(e, _s=wslot):
                            return e.tensor_tensor(out=wlo[_s][:], in0=wstg[_s][:], in1=whi[_s][:], op=ALU.subtract)
                        sch.sig("vector", wsub)
                        ev_wlo = sch.cnt["vector"]
                        sch.setr("wstgA%d" % wslot, "vector")
                        sch.wait("gpsimd", "vector", ev_wlo)
                        def stwh(e, _f=f, _s=wslot):
                            return e.dma_start(out=wqkv_hi[_f], in_=whi[_s][:])
                        sch.dma("gpsimd", stwh, "sWA%d" % wslot)
                        def stwl(e, _f=f, _s=wslot):
                            return e.dma_start(out=wqkv_lo[_f], in_=wlo[_s][:])
                        sch.dma("gpsimd", stwl, "sWA%d" % wslot)
                        sch.setr("wpairAs%d" % wslot, "sWA%d" % wslot)
                        wdep = ("vector", max(ev_xsplit, ev_wlo))
                    else:
                        sch.use("sync", "wpairA%d" % wslot)
                        sch.use("sync", "wpairAs%d" % wslot)
                        def ldwh(e, _f=f, _s=wslot):
                            return e.dma_start(out=whi[_s][:], in_=wqkv_hi[_f])
                        sch.dma("sync", ldwh, "wA%d" % wslot)
                        def ldwl(e, _f=f, _s=wslot):
                            return e.dma_start(out=wlo[_s][:], in_=wqkv_lo[_f])
                        sch.dma("sync", ldwl, "wA%d" % wslot)
                        ev_w = sch.cnt["wA%d" % wslot]
                        sch.setr("wpairAs%d" % wslot, "wA%d" % wslot)
                        wdep = ("wA%d" % wslot, ev_w)

                    if pending is not None and (f % 2 == 0):
                        emit_rope(pending)
                        pending = None

                    sch.wait("tensor", wdep[0], wdep[1])
                    sch.use("tensor", "psA%d" % bank)
                    if c == 0 and f == 0:
                        sch.wait("tensor", "vector", ve_ones)
                    pbank = PS[bank]
                    nmm = NEC * 3
                    k = 0
                    for ec in range(NEC):
                        if ec % 4 == 0 and f == 0:
                            sch.wait("tensor", "vector", ev_xsplit_g[ec // 4])
                        for (wo, xo) in ((whi[wslot], xhi), (whi[wslot], xlo), (wlo[wslot], xhi)):
                            k += 1
                            def mm(e, _p=pbank, _w=wo, _x=xo, _ec=ec, _s=(k == 1), _e=(k == nmm)):
                                return e.matmul(_p[:, 0:CHA], _w[:, _ec * 128:(_ec + 1) * 128],
                                                _x[:, _ec * CHA:(_ec + 1) * CHA], start=_s, stop=_e)
                            if k == nmm:
                                sch.sig("tensor", mm)
                            else:
                                sch.emit("tensor", mm)
                    ev_mm = sch.cnt["tensor"]
                    sch.setr("wpairA%d" % wslot, "tensor")
                    if f == NFC_QKV - 1:
                        sch.setr("xpairA", "tensor")

                    if f < HQ + HKV:
                        if f % 2 == 1:
                            isq = f < HQ
                            rt = rQ if isq else rK
                            j = (f // 2) if isq else ((f - HQ) // 2)
                            scrh = qT_hi if isq else kT_hi
                            scrl = qT_lo if isq else kT_lo
                            pending = (PS[(bank + 3) % 4], PS[bank], rt, j, scrh, scrl, ev_mm, ev_r, t0)
                    else:
                        if pending is not None:
                            emit_rope(pending)
                            pending = None
                        sch.wait("scalar", "tensor", ev_mm)
                        sch.use("scalar", "vstA%d" % wslot)
                        def vcopy(e, _p=pbank, _v=vst[wslot]):
                            return e.copy(out=_v[:], in_=_p[:, 0:CHA])
                        sch.sig("scalar", vcopy)
                        sch.setr("psA%d" % bank, "scalar")
                        ev_vc = sch.cnt["scalar"]
                        sch.wait("tensor", "scalar", ev_vc)
                        sch.use("tensor", "psVT%d" % wslot)
                        if c == 0 and f == HQ + HKV:
                            sch.wait("tensor", "cN", ld_consts)
                        ptb = PS[4 + wslot]
                        for tb in range(CHA // 128):
                            def vtr(e, _p=ptb, _v=vst[wslot], _tb=tb):
                                return e.transpose(_p[:, _tb * 128:(_tb + 1) * 128],
                                                   _v[:, _tb * 128:(_tb + 1) * 128], sid[:])
                            if tb == CHA // 128 - 1:
                                sch.sig("tensor", vtr)
                            else:
                                sch.emit("tensor", vtr)
                        sch.setr("vstA%d" % wslot, "tensor")
                        ev_vt = sch.cnt["tensor"]
                        sch.wait("vector", "tensor", ev_vt)
                        sch.use("vector", "vtkA%d" % wslot)
                        def vevac(e, _p=ptb, _v=vtk[wslot]):
                            return e.tensor_copy(_v[:], _p[:, 0:CHA])
                        sch.sig("vector", vevac)
                        sch.setr("psVT%d" % wslot, "vector")
                        ev_ve = sch.cnt["vector"]
                        sch.wait("scalar", "vector", ev_ve)
                        sch.use("scalar", "vpairA%d" % wslot)
                        def vcast2(e, _s=wslot):
                            return e.copy(out=vhs[_s][:], in_=vtk[_s][:])
                        sch.sig("scalar", vcast2)
                        ev_vc2 = sch.cnt["scalar"]
                        sch.wait("vector", "scalar", ev_vc2)
                        def vsub2(e, _s=wslot):
                            e.drain()
                            return e.tensor_tensor(out=vls[_s][:], in0=vtk[_s][:], in1=vhs[_s][:], op=ALU.subtract)
                        sch.sig("vector", vsub2)
                        ev_vs2 = sch.cnt["vector"]
                        sch.setr("vtkA%d" % wslot, "vector")
                        sch.wait("gpsimd", "vector", ev_vs2)
                        for (vb, vscr) in ((vhs, v_hi), (vls, v_lo)):
                            def stv(e, _v=vb[wslot], _f=f - HQ - HKV, _t0=t0, _s=vscr):
                                dst = _s[_t0:_t0 + CHA, _f * 128:(_f + 1) * 128].rearrange("(g p) d -> p g d", p=128)
                                return e.dma_start(out=dst, in_=_v[:].rearrange("p (g d) -> p g d", d=128))
                            sch.dma("gpsimd", stv, "sV%d" % wslot)
                        sch.setr("vpairA%d" % wslot, "sV%d" % wslot)
                if pending is not None:
                    emit_rope(pending)
                    pending = None

        sch.barrier()
        sch.res.clear()

        with ExitStack() as bctx:
            KTh = bctx.enter_context(nc.sbuf_tensor("KTh", [128, HKV * S], BF16))
            KTl = bctx.enter_context(nc.sbuf_tensor("KTl", [128, HKV * S], BF16))
            Vh = bctx.enter_context(nc.sbuf_tensor("Vh", [128, (S // 128) * HKV * D], BF16))
            Vl = bctx.enter_context(nc.sbuf_tensor("Vl", [128, (S // 128) * HKV * D], BF16))
            pF = [bctx.enter_context(nc.sbuf_tensor("pF%d" % i, [128, QT], F32)) for i in range(2)]
            ph = [bctx.enter_context(nc.sbuf_tensor("phB%d" % i, [128, 8 * QT], BF16)) for i in range(2)]
            pl = [bctx.enter_context(nc.sbuf_tensor("plB%d" % i, [128, 8 * QT], BF16)) for i in range(2)]
            fin_heads = []
            fin_q0 = []
            qbh = [bctx.enter_context(nc.sbuf_tensor("qbhB%d" % i, [128, QT], BF16)) for i in range(2)]
            qbl = [bctx.enter_context(nc.sbuf_tensor("qblB%d" % i, [128, QT], BF16)) for i in range(2)]
            srec = bctx.enter_context(nc.sbuf_tensor("srecB", [1, QT], F32))
            sbc = bctx.enter_context(nc.sbuf_tensor("sbcB", [128, QT], F32))
            atst = [bctx.enter_context(nc.sbuf_tensor("atstB%d" % i, [128, QT], F32)) for i in range(2)]

            def finalize(pend):
                qh, pslot, ev_rec, ev_pv, q0 = pend
                sch.wait("tensor", "vector", ev_rec)
                sch.use("tensor", "psSC3")
                sch.sig("tensor", lambda e: e.matmul(PS[3][:, 0:QT], sones[0:1, 0:128], srec[0:1, :],
                                                     start=True, stop=True))
                ev_bc = sch.cnt["tensor"]
                sch.setr("srecB", "tensor")
                sch.wait("scalar", "tensor", ev_bc)
                sch.use("scalar", "sbcB")
                sch.sig("scalar", lambda e: e.copy(out=sbc[:], in_=PS[3][:, 0:QT]))
                ev_bce = sch.cnt["scalar"]
                sch.setr("psSC3", "scalar")
                sch.wait("vector", "scalar", ev_bce)
                sch.wait("vector", "tensor", ev_pv)
                sch.use("vector", "atstB%d" % pslot)
                def nrm(e, _s=pslot):
                    return e.tensor_tensor(out=atst[_s][:], in0=PS[4 + _s][:, 0:QT], in1=sbc[:], op=ALU.mult)
                sch.sig("vector", nrm)
                ev_nrm = sch.cnt["vector"]
                sch.setr("psAT%d" % pslot, "vector")
                sch.setr("sbcB", "vector")
                sch.wait("gpsimd", "vector", ev_nrm)
                def sta(e, _s=pslot, _qh=qh, _q0=q0):
                    return e.dma_start(out=attnT_scr[_qh * 128:(_qh + 1) * 128, _q0:_q0 + QT], in_=atst[_s][:])
                sch.dma("gpsimd", sta, "sAt%d" % pslot)
                sch.setr("atstB%d" % pslot, "sAt%d" % pslot)

            pend_fin = None
            pvq = []
            open_banks = {}
            done_head = []

            def pop_pv(nmax=1, min_keep=3):
                n = 0
                while len(pvq) > min_keep and n < nmax:
                    h_, kc_, ps_, fi_, la_, nk_, ev_ = pvq.pop(0)
                    sch.wait("tensor", "vector", ev_)
                    if fi_:
                        sch.use("tensor", "psAT%d" % ps_)
                        sch.use("tensor", "psSUM%d" % ps_)
                    emit_pv_sum2(h_, kc_, ps_, fi_, la_)
                    sch.setr("pp%d_%d" % (ps_, kc_ % 8), "tensor")
                    n += 1
                    if la_:
                        ev_pv_ = sch.cnt["tensor"]
                        sch.wait("vector", "tensor", ev_pv_)
                        sch.use("vector", "srecB")
                        sch.sig("vector", lambda e, _s=ps_: e.reciprocal(out=srec[0:1, :], in_=PS[6 + _s][0:1, 0:QT]))
                        sch.setr("psSUM%d" % ps_, "vector")
                        done_head.append((ps_, sch.cnt["vector"], ev_pv_))
                return n

            def emit_pv_sum2(h, kc, pslot, first, last):
                def pv(e, _h=h, _kc=kc, _s=pslot, _st=first, _sp=last):
                    vsl = slice(_kc * HKV * D + _h * 128, _kc * HKV * D + (_h + 1) * 128)
                    psl = slice((_kc % 8) * QT, (_kc % 8 + 1) * QT)
                    e.matmul(PS[4 + _s][:, 0:QT], Vh[:, vsl], ph[_s][:, psl], start=_st, stop=False)
                    e.matmul(PS[4 + _s][:, 0:QT], Vh[:, vsl], pl[_s][:, psl], start=False, stop=False)
                    return e.matmul(PS[4 + _s][:, 0:QT], Vl[:, vsl], ph[_s][:, psl], start=False, stop=_sp)
                if last:
                    sch.sig("tensor", pv)
                else:
                    sch.emit("tensor", pv)
                def osum(e, _kc=kc, _s=pslot, _st=first, _sp=last):
                    psl = slice((_kc % 8) * QT, (_kc % 8 + 1) * QT)
                    e.matmul(PS[6 + _s][0:1, 0:QT], sonesb[:, 0:1], ph[_s][:, psl], start=_st, stop=False)
                    return e.matmul(PS[6 + _s][0:1, 0:QT], sonesb[:, 0:1], pl[_s][:, psl], start=False, stop=_sp)
                if last:
                    sch.sig("tensor", osum)
                else:
                    sch.emit("tensor", osum)

            for qt in range(NQT):
                q0 = qt * QT
                nkc = (q0 + QT) // 128
                sch.use("sync", "kvGate")
                for h in range(HKV):
                    for (scrh, dsth) in ((kT_hi, KTh), (kT_lo, KTl)):
                        def ldk(e, _h=h, _q0=q0, _s=scrh, _d=dsth):
                            return e.dma_start(out=_d[:, _h * S + _q0:_h * S + _q0 + QT],
                                               in_=_s[_h * 128:(_h + 1) * 128, _q0:_q0 + QT])
                        sch.dma("sync", ldk, "kvB")
                for kc in range(4 * qt, 4 * qt + 4):
                    for (scrv, dstv) in ((v_hi, Vh), (v_lo, Vl)):
                        def ldv(e, _kc=kc, _s=scrv, _d=dstv):
                            return e.dma_start(out=_d[:, _kc * HKV * D:(_kc + 1) * HKV * D],
                                               in_=_s[_kc * 128:(_kc + 1) * 128, :])
                        sch.dma("sync", ldv, "kvB")
                ev_kv = sch.cnt["kvB"]

                for qh in range(HQ):
                    h = qh // GRP
                    pslot = qh % 2
                    while done_head:
                        ps_, ev_rec, ev_pv = done_head.pop(0)
                        finalize((fin_heads.pop(0), ps_, ev_rec, ev_pv, fin_q0.pop(0)))
                    sch.use("sync", "qbufB%d" % pslot)
                    for (scrq, dstq) in ((qT_hi, qbh[pslot]), (qT_lo, qbl[pslot])):
                        def ldq(e, _qh=qh, _q0=q0, _s=scrq, _d=dstq):
                            return e.dma_start(out=_d[:], in_=_s[_qh * 128:(_qh + 1) * 128, _q0:_q0 + QT])
                        sch.dma("sync", ldq, "qB%d" % pslot)
                    ev_q = sch.cnt["qB%d" % pslot]

                    sch.wait("tensor", "kvB", ev_kv)
                    sch.wait("tensor", "qB%d" % pslot, ev_q)
                    for kc in range(nkc):
                        sch.use("tensor", "psSC%d" % (kc % 4))
                        def qk(e, _h=h, _kc=kc, _s=pslot, _b=PS[kc % 4]):
                            sl = slice(_h * S + _kc * 128, _h * S + (_kc + 1) * 128)
                            e.matmul(_b[:, 0:QT], KTh[:, sl], qbh[_s][:], start=True, stop=False)
                            e.matmul(_b[:, 0:QT], KTh[:, sl], qbl[_s][:], start=False, stop=False)
                            return e.matmul(_b[:, 0:QT], KTl[:, sl], qbh[_s][:], start=False, stop=True)
                        sch.sig("tensor", qk)
                        ev_qk = sch.cnt["tensor"]
                        sch.setr("qbufB%d" % pslot, "tensor")
                        if kc == 0 and qh == 0:
                            sch.setr("kvGate", "tensor")
                        pop_pv(1, min_keep=(3 if qt == 0 else 5))
                        fslot = kc % 2
                        sch.use("scalar", "pF%d" % fslot)
                        sch.wait("scalar", "tensor", ev_qk)
                        def ex(e, _b=PS[kc % 4], _o=pF[fslot]):
                            return e.activation(out=_o[:], in_=_b[:, 0:QT], func=AF.Exp, bias=0.0, scale=1.0)
                        sch.sig("scalar", ex)
                        ev_exp = sch.cnt["scalar"]
                        sch.setr("psSC%d" % (kc % 4), "scalar")
                        sch.wait("vector", "scalar", ev_exp)
                        sch.use("vector", "pp%d_%d" % (pslot, kc % 8))
                        diag = kc >= 4 * qt
                        delta = kc * 128 - q0
                        def vsplit(e, _o=pF[fslot], _kc=kc, _s=pslot, _diag=diag, _d=delta):
                            psl = slice((_kc % 8) * QT, (_kc % 8 + 1) * QT)
                            if _diag:
                                e.tensor_tensor(out=_o[:], in0=_o[:],
                                                in1=sstair[:, 384 - _d:384 - _d + QT], op=ALU.mult)
                                e.drain()
                            e.tensor_copy(ph[_s][:, psl], _o[:])
                            e.drain()
                            return e.tensor_tensor(out=pl[_s][:, psl], in0=_o[:], in1=ph[_s][:, psl], op=ALU.subtract)
                        sch.sig("vector", vsplit)
                        ev_sub = sch.cnt["vector"]
                        sch.setr("pF%d" % fslot, "vector")
                        pvq.append((h, kc, pslot, kc == 0, kc == nkc - 1, nkc, ev_sub))
                    fin_heads.append(qh)
                    fin_q0.append(q0)
            while pvq or done_head:
                if done_head:
                    ps_, ev_rec, ev_pv = done_head.pop(0)
                    finalize((fin_heads.pop(0), ps_, ev_rec, ev_pv, fin_q0.pop(0)))
                else:
                    pop_pv(1, min_keep=0)

        sch.barrier()
        sch.res.clear()

        with ExitStack() as cctx:
            astg = [cctx.enter_context(nc.sbuf_tensor("astgC%d" % i, [128, 4 * CHC], F32)) for i in range(2)]
            ahi = cctx.enter_context(nc.sbuf_tensor("ahiC", [128, NEC * CHC], BF16))
            alo = cctx.enter_context(nc.sbuf_tensor("aloC", [128, NEC * CHC], BF16))
            xTc2 = cctx.enter_context(nc.sbuf_tensor("xTc2C", [128, NEC * CHC], F32))
            wstgc = cctx.enter_context(nc.sbuf_tensor("wstgC", [128, NEC * 128], F32))
            wchi = [cctx.enter_context(nc.sbuf_tensor("wchiC%d" % i, [128, NEC * 128], BF16)) for i in range(2)]
            wclo = [cctx.enter_context(nc.sbuf_tensor("wcloC%d" % i, [128, NEC * 128], BF16)) for i in range(2)]
            sqst = [cctx.enter_context(nc.sbuf_tensor("sqstC%d" % i, [128, CHC], F32)) for i in range(2)]
            m_s = cctx.enter_context(nc.sbuf_tensor("m_sC", [128, CHC], F32))

            def c_loads(c):
                t0 = c * CHC
                ev_g = {}
                for grp_ in range(8):
                    ast = astg[grp_ % 2]
                    sch.use("sync", "astgC%d" % (grp_ % 2))
                    if grp_ < 2:
                        sch.use("sync", "apairC")
                    def lda2(e, _g=grp_, _t0=t0, _a=ast):
                        src = attnT_scr[_g * 512:(_g + 1) * 512, _t0:_t0 + CHC].rearrange("(g p) t -> p g t", p=128)
                        return e.dma_start(out=_a[:].rearrange("p (g t) -> p g t", t=CHC), in_=src)
                    sch.dma("sync", lda2, "ioC%d" % (grp_ % 2))
                    ev_lda = sch.cnt["ioC%d" % (grp_ % 2)]
                    sch.wait("scalar", "ioC%d" % (grp_ % 2), ev_lda)
                    def acast(e, _g=grp_, _a=ast):
                        return e.copy(out=ahi[:, _g * 4 * CHC:(_g + 1) * 4 * CHC], in_=_a[:])
                    sch.sig("scalar", acast)
                    ev_ac = sch.cnt["scalar"]
                    sch.wait("vector", "scalar", ev_ac)
                    def asub(e, _g=grp_, _a=ast):
                        return e.tensor_tensor(out=alo[:, _g * 4 * CHC:(_g + 1) * 4 * CHC], in0=_a[:],
                                               in1=ahi[:, _g * 4 * CHC:(_g + 1) * 4 * CHC], op=ALU.subtract)
                    sch.sig("vector", asub)
                    sch.setr("astgC%d" % (grp_ % 2), "vector")
                    ev_g[grp_] = sch.cnt["vector"]
                return ev_g

            def x_loads(c):
                t0 = c * CHC
                sch.use("sync", "xTc2C")
                for grp_ in range(4):
                    def ldxr(e, _g=grp_, _t0=t0):
                        src = xT[_g * 1024:(_g + 1) * 1024, _t0:_t0 + CHC].rearrange("(g p) t -> p g t", p=128)
                        d = xTc2[:, _g * 8 * CHC:(_g + 1) * 8 * CHC].rearrange("p (g t) -> p g t", t=CHC)
                        return e.dma_start(out=d, in_=src)
                    sch.dma("sync", ldxr, "xrC")
                return sch.cnt["xrC"]

            def w_chain(f, slot, cache):
                if cache:
                    sch.use("sync", "wstgC")
                    def ldw2(e, _f=f):
                        return e.dma_start(out=wstgc[:], in_=wout[_f].rearrange("p a b -> p (a b)"))
                    sch.dma("sync", ldw2, "wC")
                    ev_w = sch.cnt["wC"]
                    sch.wait("scalar", "wC", ev_w)
                    sch.use("scalar", "wpairC%d" % slot)
                    sch.use("scalar", "wpairCs%d" % slot)
                    def wcast2(e, _s=slot):
                        return e.copy(out=wchi[_s][:], in_=wstgc[:])
                    sch.sig("scalar", wcast2)
                    ev_whi = sch.cnt["scalar"]
                    sch.wait("vector", "scalar", ev_whi)
                    def wsub2(e, _s=slot):
                        return e.tensor_tensor(out=wclo[_s][:], in0=wstgc[:], in1=wchi[_s][:], op=ALU.subtract)
                    sch.sig("vector", wsub2)
                    sch.setr("wstgC", "vector")
                    ev_v = sch.cnt["vector"]
                    sch.wait("gpsimd", "vector", ev_v)
                    def stwh(e, _f=f, _s=slot):
                        return e.dma_start(out=wout_hi[_f], in_=wchi[_s][:])
                    sch.dma("gpsimd", stwh, "sWC%d" % slot)
                    def stwl(e, _f=f, _s=slot):
                        return e.dma_start(out=wout_lo[_f], in_=wclo[_s][:])
                    sch.dma("gpsimd", stwl, "sWC%d" % slot)
                    sch.setr("wpairCs%d" % slot, "sWC%d" % slot)
                    return ("vector", ev_v)
                else:
                    sch.use("sync", "wpairC%d" % slot)
                    sch.use("sync", "wpairCs%d" % slot)
                    def ldwh(e, _f=f, _s=slot):
                        return e.dma_start(out=wchi[_s][:], in_=wout_hi[_f])
                    sch.dma("sync", ldwh, "wC%d" % slot)
                    def ldwl(e, _f=f, _s=slot):
                        return e.dma_start(out=wclo[_s][:], in_=wout_lo[_f])
                    sch.dma("sync", ldwl, "wC%d" % slot)
                    ev_w = sch.cnt["wC%d" % slot]
                    sch.setr("wpairCs%d" % slot, "wC%d" % slot)
                    return ("wC%d" % slot, ev_w)

            nxt = c_loads(0)
            nxt_xr = x_loads(0)
            pre_w = None
            for c in range(NCHC):
                t0 = c * CHC
                ev_asplit = nxt
                ev_xr = nxt_xr

                for f in range(NFC_OUT):
                    slot = f % 2
                    if f < 2 and pre_w is not None:
                        wdep = pre_w[f]
                    else:
                        wdep = w_chain(f, slot, c == 0)

                    sch.wait("tensor", wdep[0], wdep[1])
                    sch.use("tensor", "psC%d" % slot)
                    nmm = NEC * 3
                    k = 0
                    for ec in range(NEC):
                        if ec % 4 == 0 and f == 0:
                            sch.wait("tensor", "vector", ev_asplit[ec // 4])
                        for (wo, xo) in ((wchi[slot], ahi), (wchi[slot], alo), (wclo[slot], ahi)):
                            k += 1
                            def mm2(e, _p=PS[slot], _w=wo, _x=xo, _ec=ec, _s=(k == 1), _e=(k == nmm)):
                                return e.matmul(_p[:, 0:CHC], _w[:, _ec * 128:(_ec + 1) * 128],
                                                _x[:, _ec * CHC:(_ec + 1) * CHC], start=_s, stop=_e)
                            if k == nmm:
                                sch.sig("tensor", mm2)
                            else:
                                sch.emit("tensor", mm2)
                    ev_mm = sch.cnt["tensor"]
                    sch.setr("wpairC%d" % slot, "tensor")
                    if f == NFC_OUT - 1:
                        sch.setr("apairC", "tensor")

                    sch.wait("vector", "tensor", ev_mm)
                    if f < 2:
                        sch.wait("vector", "xrC", ev_xr)
                    sch.use("vector", "sqstC%d" % slot)
                    def evc(e, _p=PS[slot], _f=f, _sq=sqst[slot]):
                        xf = xTc2[:, _f * CHC:(_f + 1) * CHC]
                        e.tensor_tensor(out=xf, in0=_p[:, 0:CHC], in1=xf, op=ALU.add)
                        e.drain()
                        return e.tensor_tensor(out=_sq[:], in0=xf, in1=xf, op=ALU.mult)
                    sch.sig("vector", evc)
                    ev_ev = sch.cnt["vector"]
                    sch.setr("psC%d" % slot, "vector")
                    sch.wait("tensor", "vector", ev_ev)
                    def lnm(e, _f=f, _sq=sqst[slot], _s=(f == 0), _e=(f == NFC_OUT - 1)):
                        e.matmul(PS[2][:, 0:CHC], sones[:], xTc2[:, _f * CHC:(_f + 1) * CHC], start=_s, stop=_e)
                        return e.matmul(PS[3][:, 0:CHC], sones[:], _sq[:], start=_s, stop=_e)
                    sch.sig("tensor", lnm)
                    sch.setr("sqstC%d" % slot, "tensor")
                ev_ln = sch.cnt["tensor"]

                sch.wait("vector", "tensor", ev_ln)
                def stats(e):
                    e.tensor_scalar(out=m_s[:], in0=PS[2][:, 0:CHC], scalar1=1.0 / E, scalar2=None, op0=ALU.mult)
                    e.tensor_scalar(out=sqst[0][:], in0=PS[3][:, 0:CHC], scalar1=1.0 / E, scalar2=None, op0=ALU.mult)
                    e.drain()
                    e.tensor_tensor(out=sqst[1][:], in0=m_s[:], in1=m_s[:], op=ALU.mult)
                    e.drain()
                    return e.tensor_tensor(out=sqst[0][:], in0=sqst[0][:], in1=sqst[1][:], op=ALU.subtract)
                sch.sig("vector", stats)
                ev_stats = sch.cnt["vector"]
                sch.wait("scalar", "vector", ev_stats)
                def rstd(e):
                    e.activation(out=sqst[1][:], in_=sqst[0][:], func=AF.Ln, bias=seps[:, 0:1], scale=1.0)
                    e.drain()
                    return e.activation(out=sqst[1][:], in_=sqst[1][:], func=AF.Exp, bias=0.0, scale=-0.5)
                sch.sig("scalar", rstd)
                ev_rstd = sch.cnt["scalar"]

                if c + 1 < NCHC:
                    nxt = c_loads(c + 1)
                    pre_w = [w_chain(0, 0, False), w_chain(1, 1, False)]

                def nsub(e):
                    for _f in range(NFC_OUT):
                        xf = xTc2[:, _f * CHC:(_f + 1) * CHC]
                        e.tensor_tensor(out=xf, in0=xf, in1=m_s[:], op=ALU.subtract)
                    return e.drain()
                sch.sig("vector", nsub)
                sch.wait("vector", "scalar", ev_rstd)
                def nmul(e):
                    for _f in range(NFC_OUT):
                        xf = xTc2[:, _f * CHC:(_f + 1) * CHC]
                        e.tensor_tensor(out=xf, in0=xf, in1=sqst[1][:], op=ALU.mult)
                    return e.drain()
                sch.sig("vector", nmul)
                evn = sch.cnt["vector"]
                sch.wait("gpsimd", "vector", evn)
                for grp_ in range(4):
                    def stx1(e, _g=grp_, _t0=t0):
                        dst = x1T_scr[_g * 1024:(_g + 1) * 1024, _t0:_t0 + CHC].rearrange("(g p) t -> p g t", p=128)
                        s = xTc2[:, _g * 8 * CHC:(_g + 1) * 8 * CHC].rearrange("p (g t) -> p g t", t=CHC)
                        return e.dma_start(out=dst, in_=s)
                    sch.dma("gpsimd", stx1, "sX%d" % (c % 2))
                sch.setr("xTc2C", "sX%d" % (c % 2))
                if c + 1 < NCHC:
                    nxt_xr = x_loads(c + 1)
        sch.barrier()
        sch.res.clear()

        with ExitStack() as dctx:
            x1c2 = [dctx.enter_context(nc.sbuf_tensor("x1cD%d" % i, [128, NEC * CHD], F32)) for i in range(2)]
            abuf = [dctx.enter_context(nc.sbuf_tensor("abufD%d" % i, [128, NEC * 128], F32)) for i in range(2)]
            bbuf = [dctx.enter_context(nc.sbuf_tensor("bbufD%d" % i, [128, NNC * 128], F32)) for i in range(2)]
            zg = dctx.enter_context(nc.sbuf_tensor("zgD", [128, NNC * CHD], F32))
            gx = dctx.enter_context(nc.sbuf_tensor("gxD", [128, NNC * CHD], F32))
            qrT = dctx.enter_context(nc.sbuf_tensor("qrTD", [16, CHD], F32))
            scT = dctx.enter_context(nc.sbuf_tensor("scTD", [64, CHD], F32))
            rtok = dctx.enter_context(nc.sbuf_tensor("rtokD", [128, (CHD // 128) * NEXP], F32))
            rwork = dctx.enter_context(nc.sbuf_tensor("rworkD", [128, NEXP], F32))
            rcmp = dctx.enter_context(nc.sbuf_tensor("rcmpD", [128, NEXP], F32))
            m1n = dctx.enter_context(nc.sbuf_tensor("m1nD", [128, 1], F32))
            thr = dctx.enter_context(nc.sbuf_tensor("thrD", [128, 1], F32))
            eall = dctx.enter_context(nc.sbuf_tensor("eallD", [128, NEXP], F32))
            gsum = dctx.enter_context(nc.sbuf_tensor("gsumD", [128, 1], F32))
            grec = dctx.enter_context(nc.sbuf_tensor("grecD", [128, 1], F32))
            gtm = dctx.enter_context(nc.sbuf_tensor("gtmD", [128, (CHD // 128) * NEXP], F32))
            gT = dctx.enter_context(nc.sbuf_tensor("gTD", [64, CHD], F32))
            sqd = [dctx.enter_context(nc.sbuf_tensor("sqdD%d" % i, [128, CHD], F32)) for i in range(2)]
            m_d = dctx.enter_context(nc.sbuf_tensor("m_dD", [128, CHD], F32))
            r_d = dctx.enter_context(nc.sbuf_tensor("r_dD", [128, CHD], F32))
            t_d = dctx.enter_context(nc.sbuf_tensor("t_dD", [128, CHD], F32))

            for c in range(NCHD):
                t0 = c * CHD
                par = c % 2
                x1c = x1c2[par]
                sch.use("sync", "x1cDt%d" % par)
                sch.use("sync", "x1cDv%d" % par)
                for grp_ in range(4):
                    def ldd(e, _g=grp_, _t0=t0, _x=x1c):
                        src = x1T_scr[_g * 1024:(_g + 1) * 1024, _t0:_t0 + CHD].rearrange("(g p) t -> p g t", p=128)
                        d = _x[:, _g * 8 * CHD:(_g + 1) * 8 * CHD].rearrange("p (g t) -> p g t", t=CHD)
                        return e.dma_start(out=d, in_=src)
                    sch.dma("sync", ldd, "xD%d" % par)
                ev_xd = sch.cnt["xD%d" % par]

                sch.wait("tensor", "xD%d" % par, ev_xd)
                sch.wait("tensor", "cN", ld_consts)
                sch.use("tensor", "psD0")
                for ec in range(NEC):
                    def rq(e, _ec=ec, _x=x1c, _s=(ec == 0), _e=(ec == NEC - 1)):
                        return e.matmul(PS[0][0:16, 0:CHD], swq[:, _ec * 16:(_ec + 1) * 16],
                                        _x[:, _ec * CHD:(_ec + 1) * CHD], start=_s, stop=_e)
                    if ec == NEC - 1:
                        sch.sig("tensor", rq)
                    else:
                        sch.emit("tensor", rq)
                ev_rq = sch.cnt["tensor"]
                sch.wait("vector", "tensor", ev_rq)
                sch.sig("vector", lambda e: e.tensor_copy(qrT[:], PS[0][0:16, 0:CHD]))
                ev_qr = sch.cnt["vector"]
                sch.setr("psD0", "vector")
                sch.wait("tensor", "vector", ev_qr)
                sch.use("tensor", "psD1")
                sch.sig("tensor", lambda e: e.matmul(PS[1][0:64, 0:CHD], skeys[:], qrT[:], start=True, stop=True))
                ev_s64 = sch.cnt["tensor"]
                sch.wait("vector", "tensor", ev_s64)
                sch.sig("vector", lambda e: e.tensor_copy(scT[:], PS[1][0:64, 0:CHD]))
                ev_sce = sch.cnt["vector"]
                sch.setr("psD1", "vector")
                sch.wait("tensor", "vector", ev_sce)
                for tb in range(CHD // 128):
                    bank = tb % 2
                    sch.use("tensor", "psD%d" % bank)
                    def trr(e, _tb=tb, _b=bank):
                        return e.transpose(PS[_b][:, 0:64], scT[:, _tb * 128:(_tb + 1) * 128], sid[0:64, 0:64])
                    sch.sig("tensor", trr)
                    evt = sch.cnt["tensor"]
                    sch.wait("vector", "tensor", evt)
                    def trre(e, _tb=tb, _b=bank):
                        return e.tensor_copy(rtok[:, _tb * NEXP:(_tb + 1) * NEXP], PS[_b][:, 0:64])
                    sch.sig("vector", trre)
                    sch.setr("psD%d" % bank, "vector")
                ev_rtok = sch.cnt["vector"]

                ev_zmm = {}
                for nc_ in range(NNC):
                    gslot = (c * NNC + nc_) % 2
                    sch.use("sync", "abufD%d" % gslot)
                    def lda(e, _n=nc_, _s=gslot):
                        return e.dma_start(out=abuf[_s][:], in_=wa[_n].rearrange("p a b -> p (a b)"))
                    sch.dma("sync", lda, "aD%d" % gslot)
                    ev_wa = sch.cnt["aD%d" % gslot]
                    sch.wait("tensor", "aD%d" % gslot, ev_wa)
                    sch.use("tensor", "psD%d" % nc_)
                    for ec in range(NEC):
                        def zmm(e, _p=PS[nc_], _w=abuf[gslot], _ec=ec, _x=x1c, _s=(ec == 0), _e=(ec == NEC - 1)):
                            return e.matmul(_p[:, 0:CHD], _w[:, _ec * 128:(_ec + 1) * 128],
                                            _x[:, _ec * CHD:(_ec + 1) * CHD], start=_s, stop=_e)
                        if ec == NEC - 1:
                            sch.sig("tensor", zmm)
                        else:
                            sch.emit("tensor", zmm)
                    ev_zmm[nc_] = sch.cnt["tensor"]
                    sch.setr("abufD%d" % gslot, "tensor")

                for tb in range(CHD // 128):
                    rt = rtok[:, tb * NEXP:(tb + 1) * NEXP]
                    def tk(e, _rt=rt):
                        e.tensor_copy(rwork[:], _rt)
                        e.tensor_reduce(out=m1n[:], in_=_rt, axis=AX.X, op=ALU.max, negate=True)
                        e.drain()
                        for i in range(TOPK):
                            e.tensor_reduce(out=thr[:], in_=rwork[:], axis=AX.X, op=ALU.max)
                            e.drain()
                            if i < TOPK - 1:
                                e.tensor_scalar(out=rcmp[:], in0=rwork[:], scalar1=thr[:], scalar2=None, op0=ALU.is_ge)
                                e.drain()
                                e.scalar_tensor_tensor(out=rwork[:], in0=rcmp[:], scalar=-3.0e38, in1=rwork[:],
                                                       op0=ALU.mult, op1=ALU.add)
                                e.drain()
                        return e.nop()
                    sch.sig("vector", tk)
                    ev_tk = sch.cnt["vector"]
                    sch.wait("scalar", "vector", ev_tk)
                    def gexp(e, _rt=rt):
                        return e.activation(out=eall[:], in_=_rt, func=AF.Exp, bias=m1n[:, 0:1], scale=1.0)
                    sch.sig("scalar", gexp)
                    ev_ge = sch.cnt["scalar"]
                    sch.wait("vector", "scalar", ev_ge)
                    def gfin(e, _rt=rt, _tb=tb):
                        e.tensor_scalar(out=rcmp[:], in0=_rt, scalar1=thr[:], scalar2=None, op0=ALU.is_ge)
                        e.drain()
                        e.scalar_tensor_tensor(out=eall[:], in0=eall[:], scalar=1.0, in1=rcmp[:],
                                               op0=ALU.mult, op1=ALU.mult, accum_out=gsum[:])
                        e.drain()
                        e.reciprocal(out=grec[:], in_=gsum[:])
                        e.drain()
                        return e.tensor_scalar(out=gtm[:, _tb * NEXP:(_tb + 1) * NEXP], in0=eall[:],
                                               scalar1=grec[:], scalar2=2.0, op0=ALU.mult, op1=ALU.mult)
                    sch.sig("vector", gfin)
                ev_gates = sch.cnt["vector"]

                sch.wait("tensor", "vector", ev_gates)
                for tb in range(CHD // 128):
                    bank = 4 + tb % 2
                    sch.use("tensor", "psD%d" % bank)
                    def gtr(e, _tb=tb, _b=bank):
                        return e.transpose(PS[_b][0:64, 0:128], gtm[:, _tb * NEXP:(_tb + 1) * NEXP], sid[:])
                    sch.sig("tensor", gtr)
                    evg = sch.cnt["tensor"]
                    sch.wait("vector", "tensor", evg)
                    def gtre(e, _tb=tb, _b=bank):
                        return e.tensor_copy(gT[:, _tb * 128:(_tb + 1) * 128], PS[_b][0:64, 0:128])
                    sch.sig("vector", gtre)
                    sch.setr("psD%d" % bank, "vector")
                ev_gT = sch.cnt["vector"]
                sch.wait("tensor", "vector", ev_gT)
                for nc_ in range(NNC):
                    bank = 4 + (nc_ % 2)
                    sch.use("tensor", "psD%d" % bank)
                    def gem(e, _n=nc_, _b=bank):
                        return e.matmul(PS[_b][:, 0:CHD], semat[:, _n * 128:(_n + 1) * 128], gT[:],
                                        start=True, stop=True)
                    sch.sig("tensor", gem)
                    evgm = sch.cnt["tensor"]
                    sch.wait("scalar", "tensor", evgm)
                    if nc_ == 0:
                        sch.use("scalar", "gxD")
                    def geme(e, _n=nc_, _b=bank):
                        return e.copy(out=gx[:, _n * CHD:(_n + 1) * CHD], in_=PS[_b][:, 0:CHD])
                    sch.sig("scalar", geme)
                    sch.setr("psD%d" % bank, "scalar")
                ev_gx = sch.cnt["scalar"]

                for nc_ in range(NNC):
                    sch.wait("vector", "tensor", ev_zmm[nc_])
                    sch.wait("vector", "scalar", ev_gx)
                    if nc_ == 0:
                        sch.use("vector", "zgD")
                    def zev(e, _p=PS[nc_], _n=nc_):
                        return e.scalar_tensor_tensor(out=zg[:, _n * CHD:(_n + 1) * CHD], in0=_p[:, 0:CHD],
                                                      scalar=1.0, in1=gx[:, _n * CHD:(_n + 1) * CHD],
                                                      op0=ALU.mult, op1=ALU.mult)
                    sch.sig("vector", zev)
                    sch.setr("psD%d" % nc_, "vector")
                ev_zg = sch.cnt["vector"]
                sch.setr("gxD", "vector")

                for f in range(NFC_OUT):
                    slot = f % 2
                    bank = 4 + slot
                    sch.use("sync", "bbufD%d" % slot)
                    def ldb(e, _f=f, _s=slot):
                        return e.dma_start(out=bbuf[_s][:], in_=wb[_f].rearrange("p a b -> p (a b)"))
                    sch.dma("sync", ldb, "bD%d" % slot)
                    ev_wb = sch.cnt["bD%d" % slot]
                    sch.wait("tensor", "bD%d" % slot, ev_wb)
                    sch.wait("tensor", "vector", ev_zg)
                    sch.use("tensor", "psD%d" % bank)
                    for n_ in range(NNC):
                        def bmm(e, _p=PS[bank], _w=bbuf[slot], _n=n_, _s=(n_ == 0), _e=(n_ == NNC - 1)):
                            return e.matmul(_p[:, 0:CHD], _w[:, _n * 128:(_n + 1) * 128],
                                            zg[:, _n * CHD:(_n + 1) * CHD], start=_s, stop=_e)
                        if n_ == NNC - 1:
                            sch.sig("tensor", bmm)
                        else:
                            sch.emit("tensor", bmm)
                    ev_b = sch.cnt["tensor"]
                    sch.setr("bbufD%d" % slot, "tensor")
                    if f == NFC_OUT - 1:
                        sch.setr("zgD", "tensor")
                    sch.wait("vector", "tensor", ev_b)
                    sch.use("vector", "sqdD%d" % slot)
                    def evo(e, _p=PS[bank], _f=f, _sq=sqd[slot], _x=x1c):
                        xf = _x[:, _f * CHD:(_f + 1) * CHD]
                        e.tensor_tensor(out=xf, in0=_p[:, 0:CHD], in1=xf, op=ALU.add)
                        e.drain()
                        return e.tensor_tensor(out=_sq[:], in0=xf, in1=xf, op=ALU.mult)
                    sch.sig("vector", evo)
                    ev_eo = sch.cnt["vector"]
                    sch.setr("psD%d" % bank, "vector")
                    sch.wait("tensor", "vector", ev_eo)
                    def lnm2(e, _f=f, _sq=sqd[slot], _x=x1c, _s=(f == 0), _e=(f == NFC_OUT - 1)):
                        e.matmul(PS[6][:, 0:CHD], sones[:], _x[:, _f * CHD:(_f + 1) * CHD], start=_s, stop=_e)
                        return e.matmul(PS[7][:, 0:CHD], sones[:], _sq[:], start=_s, stop=_e)
                    sch.sig("tensor", lnm2)
                    sch.setr("sqdD%d" % slot, "tensor")
                ev_ln2 = sch.cnt["tensor"]
                sch.setr("x1cDt%d" % par, "tensor")

                sch.wait("vector", "tensor", ev_ln2)
                def stats2(e):
                    e.tensor_scalar(out=m_d[:], in0=PS[6][:, 0:CHD], scalar1=1.0 / E, scalar2=None, op0=ALU.mult)
                    e.tensor_scalar(out=t_d[:], in0=PS[7][:, 0:CHD], scalar1=1.0 / E, scalar2=None, op0=ALU.mult)
                    e.drain()
                    e.tensor_tensor(out=r_d[:], in0=m_d[:], in1=m_d[:], op=ALU.mult)
                    e.drain()
                    return e.tensor_tensor(out=t_d[:], in0=t_d[:], in1=r_d[:], op=ALU.subtract)
                sch.sig("vector", stats2)
                ev_st2 = sch.cnt["vector"]
                sch.wait("scalar", "vector", ev_st2)
                def rstd2(e):
                    e.activation(out=r_d[:], in_=t_d[:], func=AF.Ln, bias=seps[:, 0:1], scale=1.0)
                    e.drain()
                    return e.activation(out=r_d[:], in_=r_d[:], func=AF.Exp, bias=0.0, scale=-0.5)
                sch.sig("scalar", rstd2)
                ev_rs2 = sch.cnt["scalar"]
                def nsub2(e, _x=x1c):
                    for _f in range(NFC_OUT):
                        xf = _x[:, _f * CHD:(_f + 1) * CHD]
                        e.tensor_tensor(out=xf, in0=xf, in1=m_d[:], op=ALU.subtract)
                    return e.drain()
                sch.sig("vector", nsub2)
                sch.wait("vector", "scalar", ev_rs2)
                def nmul2(e, _x=x1c):
                    for _f in range(NFC_OUT):
                        xf = _x[:, _f * CHD:(_f + 1) * CHD]
                        e.tensor_tensor(out=xf, in0=xf, in1=r_d[:], op=ALU.mult)
                    return e.drain()
                sch.sig("vector", nmul2)
                evn2 = sch.cnt["vector"]
                sch.wait("gpsimd", "vector", evn2)
                for grp_ in range(4):
                    def sto(e, _g=grp_, _t0=t0, _x=x1c):
                        dst = outT[_g * 1024:(_g + 1) * 1024, _t0:_t0 + CHD].rearrange("(g p) t -> p g t", p=128)
                        s = _x[:, _g * 8 * CHD:(_g + 1) * 8 * CHD].rearrange("p (g t) -> p g t", t=CHD)
                        return e.dma_start(out=dst, in_=s)
                    sch.dma("gpsimd", sto, "sO%d" % par)
                sch.setr("x1cDv%d" % par, "sO%d" % par)

            sch.ops["gpsimd"].append(("wait", "sO0", sch.cnt["sO0"]))
            sch.ops["gpsimd"].append(("wait", "sO1", sch.cnt["sO1"]))

        sems = {name: ctx.enter_context(nc.semaphore("sem_" + name)) for name in sch.cnt}
        with nc.Block() as block:
            sch.playback(block, sems)
    return nc



def make_consts():
    import jax
    import jax.numpy as jnp
    with jax.default_device(jax.devices("cpu")[0]):
        inv_freq32 = 1.0 / (10000.0 ** (jnp.arange(0, D, 2, dtype=jnp.float32) / D))
        ang32 = jnp.arange(S, dtype=jnp.float32)[:, None] * inv_freq32[None, :]
        cos32 = np.asarray(jnp.cos(ang32)).astype(np.float64).T
        sin32 = np.asarray(jnp.sin(ang32)).astype(np.float64).T

    def pack(c, s):
        out = np.zeros((2, 2, 128, S), dtype=np.float32)
        chi = c.astype(np.float32); shi = s.astype(np.float32)
        clo = (c - chi.astype(np.float64)).astype(np.float32)
        slo = (s - shi.astype(np.float64)).astype(np.float32)
        for half in range(2):
            out[0, 0, half * 64:(half + 1) * 64] = chi
            out[0, 1, half * 64:(half + 1) * 64] = shi
            out[1, 0, half * 64:(half + 1) * 64] = clo
            out[1, 1, half * 64:(half + 1) * 64] = slo
        return out

    ropeQ = pack(cos32 * np.float64(SCALE_QK), sin32 * np.float64(SCALE_QK))
    ropeK = pack(cos32, sin32)
    stair = (np.arange(896)[None, :] >= (np.arange(128)[:, None] + 384)).astype(np.float32)
    emat = np.zeros((64, BOT), dtype=np.float32)
    for n in range(BOT):
        emat[n // RANK, n] = 1.0
    ident = np.eye(128, dtype=np.float32)
    return ropeQ, ropeK, stair, emat, ident


def qk_row_perm():
    perm = []
    for sec_base, nh in ((0, HQ), (4096, HKV)):
        for j in range(nh // 2):
            lo = []
            hi = []
            for hh in (2 * j, 2 * j + 1):
                base = sec_base + hh * 128
                lo.extend(range(base, base + 64))
                hi.extend(range(base + 64, base + 128))
            perm.extend(lo)
            perm.extend(hi)
    perm.extend(range(5120, 6144))
    return np.asarray(perm)


def prep_weights(w_in, w_out, A_w, B_w, Wq_w, keys):
    w_in = np.asarray(w_in)[qk_row_perm()]
    wqkv = np.ascontiguousarray(w_in.T.reshape(NEC, 128, NFC_QKV, 128).transpose(2, 1, 0, 3))
    woutb = np.ascontiguousarray(w_out.T.reshape(NEC, 128, NFC_OUT, 128).transpose(2, 1, 0, 3))
    wab = np.ascontiguousarray(A_w.T.reshape(NEC, 128, NNC, 128).transpose(2, 1, 0, 3))
    wbb = np.ascontiguousarray(B_w.T.reshape(NNC, 128, NFC_OUT, 128).transpose(2, 1, 0, 3))
    wqTb = np.ascontiguousarray(Wq_w.T)
    keysTb = np.ascontiguousarray(keys.T)
    return wqkv, woutb, wab, wbb, wqTb, keysTb


def kernel(L0_attn_x, L0_attn_w_in, L0_attn_w_out, L0_ffn_A_w, L0_ffn_B_w, L0_ffn_Wq_w, L0_ffn_keys,
           dbg=False, n_cores=8):
    x = np.asarray(L0_attn_x)
    wqkv, woutb, wab, wbb, wqTb, keysTb = prep_weights(
        np.asarray(L0_attn_w_in), np.asarray(L0_attn_w_out), np.asarray(L0_ffn_A_w),
        np.asarray(L0_ffn_B_w), np.asarray(L0_ffn_Wq_w), np.asarray(L0_ffn_keys))
    ropeQ, ropeK, stair, emat, ident = make_consts()
    nc = build(dbg=dbg)
    in_maps = []
    for b in range(n_cores):
        in_maps.append({
            "xT": np.ascontiguousarray(x[b].T),
            "wqkv": wqkv, "wout": woutb, "wa": wab, "wb": wbb,
            "wqT": wqTb, "keysT": keysTb,
            "ropeQ": ropeQ, "ropeK": ropeK, "stair": stair, "emat": emat, "ident": ident,
        })
    res = run_bass_kernel_spmd(nc, in_maps, core_ids=list(range(n_cores)))
    if dbg:
        return res
    out = np.stack([np.asarray(r["outT"]).T for r in res.results], axis=0)
    return np.ascontiguousarray(out.astype(np.float32))
